# version 05
# baseline (speedup 1.0000x reference)
import numpy as np
import ml_dtypes
from contextlib import ExitStack

import concourse.bass as bass
import concourse.mybir as mybir
from concourse.bass_utils import run_bass_kernel_spmd

F32 = mybir.dt.float32
BF16 = mybir.dt.bfloat16
AF = mybir.ActivationFunctionType
ALU = mybir.AluOpType
AX = mybir.AxisListType

S = 4096
D = 1024
FF = 2048
CC = 512
H = 16
HD = 32
E = 8
MF = 2048
NT = S // 128
NCH = S // 512
EPS = 1e-5
RSQRT_HD = float(1.0 / np.sqrt(32.0))
PI = float(np.pi)

STAGE = 5
DBG = False
_IGNORE = 0

bf = ml_dtypes.bfloat16


class Sem:
    def __init__(self, handle, name):
        self.h = handle
        self.name = name
        self.n = 0


class Prog:

    ENGINES = ("sync", "gpsimd", "tensor", "vector", "scalar")

    def __init__(self, nc):
        self.nc = nc
        self.ops = {e: [] for e in self.ENGINES}

    def sem(self, ctx, name):
        return Sem(ctx.enter_context(self.nc.semaphore(name)), name)

    def emit(self, engine, fn):
        self.ops[engine].append(fn)

    def wait(self, engine, sem, val=None):
        v = sem.n if val is None else val
        if v > 0:
            self.emit(engine, lambda e, s=sem, v=v: e.wait_ge(s.h, v))

    def op(self, engine, fn, inc=None, inc_by=1):
        if inc is not None:
            inc.n += inc_by
            self.emit(engine, lambda e, f=fn, s=inc, k=inc_by: f(e).then_inc(s.h, k))
        else:
            self.emit(engine, fn)

    def dma(self, engine, out, in_, sem, **kw):
        sem.n += 16
        self.emit(engine, lambda e, o=out, i=in_, s=sem, kw=kw:
                  e.dma_start(o, i, **kw).then_inc(s.h, 16))

    def quiesce(self, bar, dma_sems):
        for e in self.ENGINES:
            for s in dma_sems:
                self.wait(e, s)
        self.barrier(bar)

    def barrier(self, bar):
        target = bar.n + len(self.ENGINES)
        for e in self.ENGINES:
            self.emit(e, lambda en, s=bar: en.nop().then_inc(s.h, 1))
        bar.n = target
        for e in self.ENGINES:
            self.wait(e, bar)

    def build(self, block):
        for name in self.ENGINES:
            ops = self.ops[name]

            def make(ops):
                def body(eng):
                    for f in ops:
                        f(eng)
                return body
            getattr(block, name)(make(ops))


def build_nc():
    nc = bass.Bass("TRN2", target_bir_lowering=False)

    def din(name, shape, dt=F32):
        return nc.dram_tensor(name, list(shape), dt, kind="ExternalInput").ap()

    x_in = din("x", (S, D))
    W_in = din("W_in", (D, FF))
    b_in = din("b_in", (FF,))
    b_sp = din("b_sp", (FF,))
    W_g = din("W_g", (FF, FF))
    b_g = din("b_g", (FF,))
    W_out = din("W_out", (FF, D))
    b_out = din("b_out", (D,))
    norm_w = din("norm_w", (D,))
    norm_b = din("norm_b", (D,))
    g1 = din("g1", (D,))
    g2 = din("g2", (D,))
    w_down = din("w_down", (D, CC))
    wq = din("wq", (CC, CC))
    wk = din("wk", (CC, CC))
    wv = din("wv", (CC, CC))
    w_up = din("w_up", (CC, D))
    router_w = din("router_w", (D, E))
    w1 = din("w1", (E, D, MF))
    w2 = din("w2", (E, MF, D))
    w3 = din("w3", (E, D, MF))

    out_t = nc.dram_tensor("out", [S, D], F32, kind="ExternalOutput").ap()
    dbg = {}
    if globals().get('DEVDUMP'):
        dbg["x1o"] = nc.dram_tensor("x1o", [S, D], F32, kind="ExternalOutput").ap()
        dbg["oo"] = nc.dram_tensor("oo", [S, CC], BF16, kind="ExternalOutput").ap()
        dbg["tto"] = nc.dram_tensor("tto", [D, S], BF16, kind="ExternalOutput").ap()
        dbg["reso"] = nc.dram_tensor("reso", [S, D], F32, kind="ExternalOutput").ap()
        dbg["cwo"] = nc.dram_tensor("cwo", [128, 8 * FF], BF16, kind="ExternalOutput").ap()
        dbg["woo"] = nc.dram_tensor("woo", [128, 16 * D], BF16, kind="ExternalOutput").ap()
        dbg["gto"] = nc.dram_tensor("gto", [128, NT * E], F32, kind="ExternalOutput").ap()
    if DBG:
        for nm, sh in [("d_cw", [128, FF]), ("d_wo", [128, D]), ("d_bcT", [128, 16]),
                       ("d_st1", [128, 64]), ("d_xnb", [128, 1024]),
                       ("d_gt", [128, 512]), ("d_x0", [128, 1024]),
                       ("d_bspf", [128, 16]), ("d_bsml", [128, 32]),
                       ("d_bgf", [128, 16]), ("d_winT", [128, 1024]),
                       ("d_qt", [128, 1024]), ("d_kt", [128, 1024]),
                       ("d_vp", [128, 1056]), ("d_ct", [128, 2048]),
                       ("d_o", [128, 512]), ("d_gate", [128, 256]),
                       ("d_tt", [128, 1024]), ("d_res", [128, 1024])]:
            dbg[nm] = nc.dram_tensor(nm, sh, F32, kind="ExternalOutput").ap()

    x1_d = nc.dram_tensor("x1_d", [S, D], F32, kind="Internal").ap()
    res_d = nc.dram_tensor("res_d", [S, D], F32, kind="Internal").ap()
    o_d = nc.dram_tensor("o_d", [S, CC], BF16, kind="Internal").ap()
    tT_d = nc.dram_tensor("tT_d", [D, S], BF16, kind="Internal").ap()

    eye_c = nc.inline_tensor(np.eye(128, dtype=bf), "eye_c").ap()
    bm_np = np.zeros((4, 128, 512), dtype=bf)
    kk = np.arange(128)[:, None]
    qq = np.arange(512)[None, :]
    for j in range(4):
        bm_np[j] = ((128 * j + kk) <= qq).astype(bf)
    bm_c = nc.inline_tensor(bm_np.reshape(4 * 128, 512), "bm_c").ap()
    inv_freq_np = (1.0 / 10000.0 ** (np.arange(0, HD, 2, dtype=np.float32) / HD)
                   ).astype(np.float32)
    invf_c = nc.inline_tensor(np.broadcast_to(inv_freq_np, (128, 16)).copy(), "invf_c").ap()
    wcol_c = nc.inline_tensor(
        np.broadcast_to(8.0 - np.arange(8, dtype=np.float32), (128, 8)).copy(),
        "wcol_c").ap()
    eyef_c = nc.inline_tensor(np.eye(33, dtype=np.float32), "eyef_c").ap()

    G = ExitStack()
    MID = ExitStack()
    P = Prog(nc)
    sLD = P.sem(G, "sLD")
    sCP = P.sem(G, "sCP")
    sPE = P.sem(G, "sPE")
    sVE = P.sem(G, "sVE")
    sSE = P.sem(G, "sSE")
    sST = P.sem(G, "sST")
    sBar = P.sem(G, "sBar")
    sX = P.sem(G, "sX")
    sWO = P.sem(G, "sWO")
    sXa = P.sem(G, "sXa")
    sXb = P.sem(G, "sXb")
    sWGa = P.sem(G, "sWGa")
    sWGb = P.sem(G, "sWGb")
    sWOb = P.sem(G, "sWOb")
    sTQ = P.sem(G, "sTQ")
    sW13 = P.sem(G, "sW13")
    sW2 = P.sem(G, "sW2")
    sW = P.sem(G, "sW")

    def mk(ctx_, name, shape, dt=F32, psum=False):
        f = nc.psum_tensor if psum else nc.sbuf_tensor
        return ctx_.enter_context(f(name, list(shape), dt))

    eye_b = mk(G, "eye_b", [128, 128], BF16)
    bm_b = mk(G, "bm_b", [128, 4 * 512], BF16)
    invf_b = mk(G, "invf_b", [128, 16], F32)
    wcol_b = mk(G, "wcol_b", [128, 8], F32)
    iotaP = mk(G, "iotaP", [128, 1], F32)
    epsb = mk(G, "epsb", [128, 1], F32)
    gate_b = mk(G, "gate_b", [128, NT * E], F32)

    P.dma("sync", eye_b[:], eye_c, sLD)
    for j in range(4):
        P.dma("sync", bm_b[:, 512 * j:512 * (j + 1)], bm_c[128 * j:128 * (j + 1), :], sLD)
    P.dma("sync", invf_b[:], invf_c, sLD)
    P.dma("sync", wcol_b[:], wcol_c, sLD)
    P.op("gpsimd", lambda e: e.iota(iotaP[:], pattern=[[0, 1]], base=0,
                                    channel_multiplier=1,
                                    allow_small_or_imprecise_dtypes=True), inc=sCP)
    P.op("gpsimd", lambda e: e.memset(epsb[:], EPS), inc=sCP)

    st1 = mk(MID, "st1", [128, 64], F32)
    sc1 = mk(MID, "sc1", [128, 1024], F32)
    sc2 = mk(MID, "sc2", [128, 1024], F32)
    MID2 = ExitStack()
    g1_b = mk(MID2, "g1_b", [128, D], F32)
    g2_b = mk(MID2, "g2_b", [128, D], F32)
    wdn_b = mk(MID2, "wdn_b", [128, 8 * CC], BF16)
    wq_b = mk(MID2, "wq_b", [128, 4 * CC], BF16)
    wk_b = mk(MID2, "wk_b", [128, 4 * CC], BF16)
    wv_b = mk(MID2, "wv_b", [128, 4 * CC], BF16)
    wup_b = mk(MID2, "wup_b", [128, 4 * D], BF16)
    rtr_b = mk(MID2, "rtr_b", [128, 8 * E], BF16)

    for name, dst, src in [("g1", g1_b, g1), ("g2", g2_b, g2)]:
        P.dma("sync", dst[0:1, :], src.rearrange("(o d) -> o d", o=1), sLD)
    for dst in (g1_b, g2_b):
        rows = 1
        while rows < 128:
            P.wait("sync", sLD)
            P.wait("sync", sST)
            P.dma("sync", dst[rows:2 * rows, :], dst[0:rows, :], sST)
            rows *= 2
    for dc in range(8):
        P.dma("gpsimd", wdn_b[:, CC * dc:CC * (dc + 1)],
              w_down[128 * dc:128 * (dc + 1), :], sCP)
    for cc in range(4):
        P.dma("gpsimd", wq_b[:, CC * cc:CC * (cc + 1)], wq[128 * cc:128 * (cc + 1), :], sCP)
        P.dma("gpsimd", wk_b[:, CC * cc:CC * (cc + 1)], wk[128 * cc:128 * (cc + 1), :], sCP)
        P.dma("gpsimd", wv_b[:, CC * cc:CC * (cc + 1)], wv[128 * cc:128 * (cc + 1), :], sCP)
        P.dma("gpsimd", wup_b[:, D * cc:D * (cc + 1)], w_up[128 * cc:128 * (cc + 1), :], sCP)
    for dc in range(8):
        P.dma("gpsimd", rtr_b[:, E * dc:E * (dc + 1)],
              router_w[128 * dc:128 * (dc + 1), :], sCP)
    P.wait("vector", sCP)
    for cc in range(4):
        P.op("vector", lambda e, cc=cc: e.tensor_scalar_mul(
            wq_b[:, CC * cc:CC * (cc + 1)], wq_b[:, CC * cc:CC * (cc + 1)], RSQRT_HD),
            inc=sVE)

    A = ExitStack()
    cw_b = mk(A, "cw_b", [128, 8 * FF], BF16)
    wo_b = mk(A, "wo_b", [128, 16 * D], BF16)
    bcT_b = mk(A, "bcT_b", [128, 16], F32)
    bout_b = mk(A, "bout_b", [128, D], F32)
    nw_b = mk(A, "nw_b", [128, D], F32)
    nb_b = mk(A, "nb_b", [128, D], F32)
    for name, dst, src_ in [("bout", bout_b, b_out), ("nw", nw_b, norm_w),
                            ("nb", nb_b, norm_b)]:
        P.dma("sync", dst[0:1, :], src_.rearrange("(o d) -> o d", o=1), sLD)
    for dst in (bout_b, nw_b, nb_b):
        rows = 1
        while rows < 128:
            P.wait("sync", sLD)
            P.wait("sync", sST)
            P.dma("sync", dst[rows:2 * rows, :], dst[0:rows, :], sST)
            rows *= 2

    with nc.sbuf_tensor("winT", [128, 16 * D], BF16) as winT, \
         nc.sbuf_tensor("bsml", [128, 32], BF16) as bsml, \
         nc.sbuf_tensor("bgf", [128, 16], F32) as bgf, \
         nc.sbuf_tensor("bspf", [128, 16], F32) as bspf:

        for c16 in range(16):
            P.dma("gpsimd", bsml[:, c16:c16 + 1],
                  b_in[128 * c16:128 * (c16 + 1)].rearrange("(p o) -> p o", o=1), sCP)
            P.dma("gpsimd", bsml[:, 16 + c16:16 + c16 + 1],
                  b_sp[128 * c16:128 * (c16 + 1)].rearrange("(p o) -> p o", o=1), sCP)
            P.dma("sync", bgf[:, c16:c16 + 1],
                  b_g[128 * c16:128 * (c16 + 1)].rearrange("(p o) -> p o", o=1), sLD)
            P.dma("sync", bspf[:, c16:c16 + 1],
                  b_sp[128 * c16:128 * (c16 + 1)].rearrange("(p o) -> p o", o=1), sLD)

        SKIP_WC = bool(globals().get('SKIP_WC'))
        with nc.sbuf_tensor("winb", [128, 2 * FF], BF16) as winb, \
             nc.psum_tensor("pp0", [128, 512], BF16) as pp0:
            for dc in range(8 if not SKIP_WC else 0):
                if dc >= 2:
                    P.wait("gpsimd", sVE, tr_evac_mark[dc - 2])
                wsem = sWGa if dc % 2 == 0 else sWGb
                P.dma("gpsimd", winb[:, FF * (dc % 2):FF * (dc % 2 + 1)],
                      W_in[128 * dc:128 * (dc + 1), :], wsem)
                P.wait("tensor", wsem)
                for j in range(16):
                    P.op("tensor", lambda e, dc=dc, j=j: e.transpose(
                        pp0[:, 128 * (j % 4):128 * (j % 4 + 1)],
                        winb[:, FF * (dc % 2) + 128 * j:FF * (dc % 2) + 128 * (j + 1)],
                        eye_b[:]), inc=sPE)
                    P.wait("vector", sPE)
                    P.op("vector", lambda e, dc=dc, j=j: e.tensor_copy(
                        winT[:, D * j + 128 * dc:D * j + 128 * (dc + 1)],
                        pp0[:, 128 * (j % 4):128 * (j % 4 + 1)]), inc=sVE)
                    P.wait("tensor", sVE)
                if dc == 0:
                    tr_evac_mark = {}
                tr_evac_mark[dc] = sVE.n

        with nc.sbuf_tensor("wgsl", [128, 2 * 512], BF16) as wgsl, \
             nc.psum_tensor("ppc0", [128, 512], F32) as ppc0, \
             nc.psum_tensor("ppc1", [128, 512], F32) as ppc1, \
             nc.psum_tensor("ppc2", [128, 512], F32) as ppc2, \
             nc.psum_tensor("ppc3", [128, 512], F32) as ppc3, \
             nc.psum_tensor("ppb0", [128, 4], F32) as ppb0, \
             nc.psum_tensor("ppb1", [128, 4], F32) as ppb1, \
             nc.psum_tensor("ppb2", [128, 4], F32) as ppb2, \
             nc.psum_tensor("ppb3", [128, 4], F32) as ppb3:
            ppcs = [ppc0, ppc1, ppc2, ppc3]
            ppbs = [ppb0, ppb1, ppb2, ppb3]
            wc_mark = {}
            for half in range(2 if not SKIP_WC else 0):
                for fgrp in range(4):
                    for fp in range(16):
                        gi = (half * 4 + fgrp) * 16 + fp
                        if gi >= 2:
                            P.wait("gpsimd", sPE, wc_mark[gi - 2])
                        wsem = sWGa if fp % 2 == 0 else sWGb
                        P.dma("gpsimd", wgsl[:, 512 * (fp % 2):512 * (fp % 2 + 1)],
                              W_g[128 * fp:128 * (fp + 1),
                                  512 * fgrp:512 * (fgrp + 1)], wsem)
                        P.wait("tensor", wsem)
                        for dtl in range(4):
                            dt = 4 * half + dtl
                            last = (dtl == 3) and (half == 1)
                            P.op("tensor", lambda e, fp=fp, dtl=dtl, dt=dt: e.matmul(
                                ppcs[dtl][:],
                                winT[:, D * fp + 128 * dt:D * fp + 128 * (dt + 1)],
                                wgsl[:, 512 * (fp % 2):512 * (fp % 2 + 1)],
                                start=(fp == 0), stop=(fp == 15)),
                                inc=(sPE if last else None))
                        if half == 0:
                            for sub in range(4):
                                P.op("tensor", lambda e, fp=fp, sub=sub, fgrp=fgrp:
                                     e.matmul(
                                         ppbs[sub][:, fgrp:fgrp + 1],
                                         wgsl[:, 512 * (fp % 2) + 128 * sub:
                                              512 * (fp % 2) + 128 * (sub + 1)],
                                         bsml[:, fp:fp + 1],
                                         start=(fp == 0), stop=(fp == 15)),
                                     inc=(sPE if sub == 3 else None))
                        wc_mark[gi] = sPE.n
                    P.wait("vector", sPE)
                    for dtl in range(4):
                        dt = 4 * half + dtl
                        P.op("vector", lambda e, dt=dt, dtl=dtl, fgrp=fgrp: e.tensor_copy(
                            cw_b[:, FF * dt + 512 * fgrp:FF * dt + 512 * (fgrp + 1)],
                            ppcs[dtl][:]), inc=sVE)
                    if half == 0:
                        for sub in range(4):
                            ftcol = 4 * fgrp + sub
                            P.op("vector", lambda e, ftcol=ftcol, sub=sub, fgrp=fgrp:
                                 e.tensor_add(
                                     bcT_b[:, ftcol:ftcol + 1],
                                     ppbs[sub][:, fgrp:fgrp + 1],
                                     bgf[:, ftcol:ftcol + 1]), inc=sVE)
                    P.wait("tensor", sVE)

        P.quiesce(sBar, [sLD, sCP, sST, sWO, sWOb, sX, sXa, sXb, sWGa, sWGb, sTQ, sW13, sW2])
        with nc.sbuf_tensor("wof", [128, 2 * D], F32) as wof:
            for fc in range(16):
                if fc >= 2:
                    P.wait("sync", sVE, wo_mark[fc - 2])
                wsem = sWO if fc % 2 == 0 else sWOb
                P.dma("sync", wof[:, D * (fc % 2):D * (fc % 2 + 1)],
                      W_out[128 * fc:128 * (fc + 1), :], wsem)
                P.wait("vector", wsem)
                P.wait("vector", sLD)
                P.op("vector", lambda e, fc=fc: e.tensor_scalar_mul(
                    wo_b[:, D * fc:D * (fc + 1)], wof[:, D * (fc % 2):D * (fc % 2 + 1)],
                    bspf[:, fc:fc + 1]), inc=sVE)
                if fc == 0:
                    wo_mark = {}
                wo_mark[fc] = sVE.n
    P.quiesce(sBar, [sLD, sCP, sST, sWO, sWOb, sX, sXa, sXb, sWGa, sWGb, sTQ, sW13, sW2])

    x0c = [mk(A, f"x0c{i}", [128, 4096], F32) for i in range(2)]
    xnT = [mk(A, f"xnT{i}", [128, 4096], BF16) for i in range(2)]
    gTt = mk(A, "gTt", [128, 16 * 512], BF16)

    aob = mk(A, "aob", [128, 1024], F32)

    xnb_s = mk(A, "xnb_s", [128, 4096], BF16)
    x1c = mk(A, "x1c", [128, 1024], F32)
    pG = [mk(A, f"pG{i}", [128, 512], psum=True) for i in range(2)]
    pA = [mk(A, f"pA{i}", [128, 512], psum=True) for i in range(4)]
    pT = mk(A, "pT", [128, 512], BF16, psum=True)

    def ln_stats(tile_ap, scol, scratch, pre_sum=None):
        if pre_sum is None:
            P.op("vector", lambda e: e.reduce_sum(st1[:, scol:scol + 1], tile_ap, AX.X),
                 inc=sVE)
        else:
            P.op("vector", lambda e: e.tensor_copy(st1[:, scol:scol + 1], pre_sum),
                 inc=sVE)
        P.op("vector", lambda e: e.scalar_tensor_tensor(
            scratch, tile_ap, 1.0, tile_ap, ALU.mult, ALU.mult,
            accum_out=st1[:, scol + 1:scol + 2]), inc=sVE)
        P.wait("vector", sVE)
        P.op("vector", lambda e: e.tensor_scalar_mul(
            st1[:, scol:scol + 1], st1[:, scol:scol + 1], 1.0 / D), inc=sVE)
        P.op("vector", lambda e: e.tensor_scalar_mul(
            st1[:, scol + 1:scol + 2], st1[:, scol + 1:scol + 2], 1.0 / D), inc=sVE)
        P.wait("vector", sVE)
        P.op("vector", lambda e: e.scalar_tensor_tensor(
            st1[:, scol + 2:scol + 3], st1[:, scol:scol + 1], 1.0,
            st1[:, scol:scol + 1], ALU.mult, ALU.mult), inc=sVE)
        P.wait("vector", sVE)
        P.op("vector", lambda e: e.tensor_sub(
            st1[:, scol + 1:scol + 2], st1[:, scol + 1:scol + 2],
            st1[:, scol + 2:scol + 3]), inc=sVE)
        P.wait("scalar", sVE)
        P.op("scalar", lambda e: e.activation(
            st1[:, scol + 2:scol + 3], st1[:, scol + 1:scol + 2], AF.Sqrt,
            bias=epsb[:]), inc=sSE)
        P.wait("vector", sSE)
        P.op("vector", lambda e: e.reciprocal(
            st1[:, scol + 1:scol + 2], st1[:, scol + 2:scol + 3]), inc=sVE)
        P.wait("vector", sVE)
        P.op("vector", lambda e: e.scalar_tensor_tensor(
            st1[:, scol:scol + 1], st1[:, scol:scol + 1], -1.0,
            st1[:, scol + 1:scol + 2], ALU.mult, ALU.mult), inc=sVE)
        P.wait("vector", sVE)

    st_mark = {}
    tr_mark = {}
    silu_mark = {}
    for c in range(NCH if not globals().get('SKIP_P1') else 0):
        xb = x0c[c % 2]
        xT = xnT[c % 2]
        if c >= 2:
            P.wait("sync", sST, st_mark[c - 2])
        xsem = sXa if c % 2 == 0 else sXb
        for tt in range(4):
            P.dma("sync", xb[:, 1024 * tt:1024 * (tt + 1)],
                  x_in[512 * c + 128 * tt:512 * c + 128 * (tt + 1), :], xsem)
        P.wait("vector", xsem)
        P.wait("scalar", xsem)
        if c >= 1:
            P.wait("scalar", sPE, tr_mark[c - 1])
        for tt in range(4):
            xt = xb[:, 1024 * tt:1024 * (tt + 1)]
            sc = 16 * tt
            ln_stats(xt, sc, sc1[:])
            P.wait("scalar", sVE)
            P.op("scalar", lambda e, xt=xt, tt=tt, sc=sc: e.activation(
                xnb_s[:, 1024 * tt:1024 * (tt + 1)], xt, AF.Identity,
                bias=st1[:, sc:sc + 1], scale=st1[:, sc + 1:sc + 2]), inc=sSE)
        P.wait("tensor", sSE)
        for tt in range(4):
            for j in range(8):
                P.op("tensor", lambda e, tt=tt, j=j: e.transpose(
                    pT[:, 128 * (j % 4):128 * (j % 4 + 1)],
                    xnb_s[:, 1024 * tt + 128 * j:1024 * tt + 128 * (j + 1)],
                    eye_b[:]), inc=sPE)
                P.wait("vector", sPE)
                P.op("vector", lambda e, tt=tt, j=j, xT=xT: e.tensor_copy(
                    xT[:, 512 * j + 128 * tt:512 * j + 128 * (tt + 1)],
                    pT[:, 128 * (j % 4):128 * (j % 4 + 1)]), inc=sVE)
                P.wait("tensor", sVE)
        tr_mark[c] = sPE.n
        for ft in range(16):
            bk = pG[ft % 2]
            if ft >= 2:
                P.wait("tensor", sSE, silu_mark[ft - 2])
            elif c >= 1:
                P.wait("tensor", sSE, silu_mark[14 + ft])
            for dc in range(8):
                P.op("tensor", lambda e, ft=ft, dc=dc, bk=bk, xT=xT: e.matmul(
                    bk[:], cw_b[:, FF * dc + 128 * ft:FF * dc + 128 * (ft + 1)],
                    xT[:, 512 * dc:512 * (dc + 1)],
                    start=(dc == 0), stop=(dc == 7)),
                    inc=(sPE if dc == 7 else None))
            P.wait("scalar", sPE)
            P.op("scalar", lambda e, ft=ft, bk=bk: e.activation(
                gTt[:, 512 * ft:512 * (ft + 1)], bk[:], AF.Silu,
                bias=bcT_b[:, ft:ft + 1]), inc=sSE)
            silu_mark[ft] = sSE.n
        P.wait("tensor", sSE)
        for tt in range(4):
            for half in range(2):
                bk = pA[(2 * tt + half) % 4]
                for ft in range(16):
                    P.op("tensor", lambda e, tt=tt, half=half, ft=ft, bk=bk: e.matmul(
                        bk[:], gTt[:, 512 * ft + 128 * tt:512 * ft + 128 * (tt + 1)],
                        wo_b[:, D * ft + 512 * half:D * ft + 512 * (half + 1)],
                        start=(ft == 0), stop=(ft == 15)),
                        inc=(sPE if ft == 15 else None))
            P.wait("vector", sPE)
            xt = xb[:, 1024 * tt:1024 * (tt + 1)]
            sc = 16 * tt
            tmp, t1 = sc1[:], sc2[:]
            for half in range(2):
                bk = pA[(2 * tt + half) % 4]
                P.op("vector", lambda e, half=half, bk=bk: e.tensor_add(
                    aob[:, 512 * half:512 * (half + 1)], bk[:],
                    bout_b[:, 512 * half:512 * (half + 1)]), inc=sVE)
            P.wait("tensor", sVE)
            P.wait("scalar", sVE)
            P.op("scalar", lambda e, xt=xt, tmp=tmp, sc=sc: e.activation(
                tmp, xt, AF.Identity, bias=st1[:, sc:sc + 1], scale=st1[:, sc + 1:sc + 2]),
                inc=sSE)
            P.wait("vector", sSE)
            P.op("vector", lambda e, t1=t1, tmp=tmp, sc=sc: e.scalar_tensor_tensor(
                t1, tmp, 1.0, aob[:], ALU.mult, ALU.add,
                accum_out=st1[:, sc + 3:sc + 4]), inc=sVE)
            P.wait("vector", sVE)
            ln_stats(t1, sc + 4, tmp, pre_sum=st1[:, sc + 3:sc + 4])
            P.wait("scalar", sVE)
            P.op("scalar", lambda e, t1=t1, tmp=tmp, sc=sc: e.activation(
                tmp, t1, AF.Identity, bias=st1[:, sc + 4:sc + 5],
                scale=st1[:, sc + 5:sc + 6]), inc=sSE)
            P.wait("vector", sSE)
            P.op("vector", lambda e, tmp=tmp, t1=t1: e.scalar_tensor_tensor(
                t1, tmp, 1.0, nw_b[:], ALU.mult, ALU.mult), inc=sVE)
            P.wait("vector", sVE)
            P.op("vector", lambda e, t1=t1, sc=sc: e.scalar_tensor_tensor(
                t1, t1, 1.0, nb_b[:], ALU.mult, ALU.add,
                accum_out=st1[:, sc + 3:sc + 4]), inc=sVE)
            P.wait("vector", sVE)
            ln_stats(t1, sc + 8, tmp, pre_sum=st1[:, sc + 3:sc + 4])
            P.wait("scalar", sVE)
            P.op("scalar", lambda e, t1=t1, tmp=tmp, sc=sc: e.activation(
                tmp, t1, AF.Identity, bias=st1[:, sc + 8:sc + 9],
                scale=st1[:, sc + 9:sc + 10]), inc=sSE)
            P.wait("vector", sSE)
            P.op("vector", lambda e, tmp=tmp, xt=xt: e.tensor_add(x1c[:], tmp, xt),
                 inc=sVE)
            P.wait("sync", sVE)
            P.dma("sync", x1_d[512 * c + 128 * tt:512 * c + 128 * (tt + 1), :],
                  x1c[:], sST)
            if STAGE == 1:
                P.dma("sync", out_t[512 * c + 128 * tt:512 * c + 128 * (tt + 1), :],
                      x1c[:], sST)
            P.wait("vector", sST)
        st_mark[c] = sST.n

    if DBG and STAGE == 1:
        P.wait("gpsimd", sSE)
        P.wait("gpsimd", sVE)
        P.wait("gpsimd", sPE)
        P.dma("gpsimd", dbg["d_cw"], cw_b[:, 0:FF], sST)
        P.dma("gpsimd", dbg["d_wo"], wo_b[:, 0:D], sST)
        P.dma("gpsimd", dbg["d_bcT"], bcT_b[:], sST)
        P.wait("gpsimd", sST)
    if globals().get('DEVDUMP'):
        P.wait("gpsimd", sVE)
        P.wait("gpsimd", sPE)
        P.wait("gpsimd", sSE)
        P.wait("gpsimd", sST)
        P.dma("gpsimd", dbg["cwo"], cw_b[:], sST)
        P.dma("gpsimd", dbg["woo"], wo_b[:], sST)
        for rb in range(8):
            P.dma("gpsimd", dbg["x1o"][512 * rb:512 * (rb + 1), :],
                  x1_d[512 * rb:512 * (rb + 1), :], sST)
        P.wait("gpsimd", sST)
    P.quiesce(sBar, [sLD, sCP, sST, sWO, sWOb, sX, sXa, sXb, sWGa, sWGb, sTQ, sW13, sW2])
    A.close()

    if STAGE >= 2:
        B = ExitStack()
        QT_s = [mk(B, f"QT{g}", [128, S], BF16) for g in range(4)]
        KT_s = [mk(B, f"KT{g}", [128, S], BF16) for g in range(4)]
        Vp_s = mk(B, "Vp_s", [128, 33 * 16 * NT], BF16)
        ones_b = mk(B, "ones_b", [128, 128], BF16)
        st2 = mk(B, "st2", [128, 16], F32)
        B2 = ExitStack()
        x1b = [mk(B2, f"x1b{i}", [128, 4096], F32) for i in range(2)]
        xf_t = mk(B2, "xf_t", [128, 1024], F32)
        h_s = mk(B2, "h_s", [128, 4096], BF16)
        hT_s = mk(B2, "hT_s", [128, 4096], BF16)
        cT_s = mk(B2, "cT_s", [128, 4 * 512], BF16)
        qr_s = mk(B2, "qr_s", [128, 512], BF16)
        rsc1 = mk(B2, "rsc1", [128, 256], F32)
        rsc2 = mk(B2, "rsc2", [128, 256], F32)
        cs_s = mk(B2, "cs_s", [128, 64], F32)
        pQK = mk(B2, "pQK", [128, 512], psum=True)
        pT2 = mk(B2, "pT2", [128, 512], BF16, psum=True)
        pV = mk(B2, "pV", [128, 512], psum=True)
        P.op("gpsimd", lambda e: e.memset(ones_b[:], 1.0), inc=sCP)
        RC = 1.0 / (2.0 * np.pi)
        MAGIC = 12582912.0
        c1 = np.float32(6.28125)
        rem = 2.0 * np.pi - float(c1)
        c2 = np.float32(rem)
        c3 = np.float32(rem - float(c2))
        tr2_mark = {}
        ld2_mark = {}
        for c in range(NCH):
            xb = x1b[c % 2]
            if c >= 2:
                P.wait("sync", sVE, ld2_mark[c - 2])
            xsem = sXa if c % 2 == 0 else sXb
            for tt in range(4):
                P.dma("sync", xb[:, 1024 * tt:1024 * (tt + 1)],
                      x1_d[512 * c + 128 * tt:512 * c + 128 * (tt + 1), :], xsem)
            P.wait("vector", xsem)
            P.wait("scalar", xsem)
            if c >= 1:
                P.wait("scalar", sPE, tr2_mark[c - 1])
                P.wait("vector", sPE, tr2_mark[c - 1])
            for tt in range(4):
                xt = xb[:, 1024 * tt:1024 * (tt + 1)]
                ln_stats(xt, 0, xf_t[:])
                P.wait("scalar", sVE)
                P.op("scalar", lambda e, xt=xt: e.activation(
                    xf_t[:], xt, AF.Identity, bias=st1[:, 0:1], scale=st1[:, 1:2]),
                    inc=sSE)
                P.wait("vector", sSE)
                P.op("vector", lambda e: e.scalar_tensor_tensor(
                    sc1[:], xf_t[:], 1.0, xf_t[:], ALU.mult, ALU.mult,
                    accum_out=st2[:, 0:1]), inc=sVE)
                P.wait("vector", sVE)
                P.op("vector", lambda e: e.tensor_scalar_mul(
                    st2[:, 0:1], st2[:, 0:1], 1.0 / D), inc=sVE)
                P.wait("scalar", sVE)
                P.op("scalar", lambda e: e.activation(
                    st2[:, 1:2], st2[:, 0:1], AF.Sqrt, bias=epsb[:]), inc=sSE)
                P.wait("vector", sSE)
                P.op("vector", lambda e: e.reciprocal(st2[:, 0:1], st2[:, 1:2]),
                     inc=sVE)
                P.wait("vector", sVE)
                P.op("vector", lambda e, tt=tt: e.scalar_tensor_tensor(
                    h_s[:, 1024 * tt:1024 * (tt + 1)], xf_t[:], st2[:, 0:1],
                    g1_b[:], ALU.mult, ALU.mult), inc=sVE)
            P.wait("tensor", sVE)
            for tt in range(4):
                for j in range(8):
                    P.op("tensor", lambda e, tt=tt, j=j: e.transpose(
                        pT2[:, 128 * (j % 4):128 * (j % 4 + 1)],
                        h_s[:, 1024 * tt + 128 * j:1024 * tt + 128 * (j + 1)],
                        eye_b[:]), inc=sPE)
                    P.wait("vector", sPE)
                    P.op("vector", lambda e, tt=tt, j=j: e.tensor_copy(
                        hT_s[:, 512 * j + 128 * tt:512 * j + 128 * (tt + 1)],
                        pT2[:, 128 * (j % 4):128 * (j % 4 + 1)]), inc=sVE)
                    P.wait("tensor", sVE)
            for ct in range(4):
                for dc in range(8):
                    P.op("tensor", lambda e, ct=ct, dc=dc: e.matmul(
                        pQK[:], wdn_b[:, CC * dc + 128 * ct:CC * dc + 128 * (ct + 1)],
                        hT_s[:, 512 * dc:512 * (dc + 1)],
                        start=(dc == 0), stop=(dc == 7)),
                        inc=(sPE if dc == 7 else None))
                P.wait("vector", sPE)
                P.op("vector", lambda e, ct=ct: e.tensor_copy(
                    cT_s[:, 512 * ct:512 * (ct + 1)], pQK[:]), inc=sVE)
                P.wait("tensor", sVE)
            for tt in range(4):
                gt = 4 * c + tt
                P.wait("vector", sSE)
                P.op("vector", lambda e, gt=gt: e.tensor_scalar(
                    st2[:, 2:3], iotaP[:], float(128 * gt), None, ALU.add), inc=sVE)
                P.wait("vector", sVE)
                P.op("vector", lambda e: e.tensor_scalar(
                    cs_s[:, 0:16], invf_b[:], st2[:, 2:3], None, ALU.mult), inc=sVE)
                P.wait("vector", sVE)
                P.op("vector", lambda e: e.tensor_scalar(
                    cs_s[:, 16:32], cs_s[:, 0:16], RC, MAGIC, ALU.mult, ALU.add),
                    inc=sVE)
                P.wait("vector", sVE)
                P.op("vector", lambda e: e.tensor_scalar(
                    cs_s[:, 16:32], cs_s[:, 16:32], -MAGIC, None, ALU.add), inc=sVE)
                P.wait("vector", sVE)
                for cc_ in (float(c1), float(c2), float(c3)):
                    P.op("vector", lambda e, cc_=cc_: e.scalar_tensor_tensor(
                        cs_s[:, 0:16], cs_s[:, 16:32], -cc_, cs_s[:, 0:16],
                        ALU.mult, ALU.add), inc=sVE)
                    P.wait("vector", sVE)
                P.op("vector", lambda e: e.tensor_scalar(
                    cs_s[:, 16:32], cs_s[:, 0:16], float(np.pi / 2), None, ALU.add),
                    inc=sVE)
                P.wait("vector", sVE)
                P.op("vector", lambda e: e.tensor_scalar(
                    cs_s[:, 32:48], cs_s[:, 16:32], float(np.pi), None, ALU.is_gt),
                    inc=sVE)
                P.wait("vector", sVE)
                P.op("vector", lambda e: e.scalar_tensor_tensor(
                    cs_s[:, 16:32], cs_s[:, 32:48], float(-2.0 * np.pi),
                    cs_s[:, 16:32], ALU.mult, ALU.add), inc=sVE)
                P.wait("scalar", sVE)
                P.op("scalar", lambda e: e.activation(
                    cs_s[:, 32:48], cs_s[:, 0:16], AF.Sin), inc=sSE)
                P.op("scalar", lambda e: e.activation(
                    cs_s[:, 48:64], cs_s[:, 16:32], AF.Sin), inc=sSE)
                P.wait("vector", sSE)
                sinB = cs_s[:, 32:48].rearrange("p (o i) -> p o i", o=1
                                                ).broadcast_to([128, 16, 16])
                cosB = cs_s[:, 48:64].rearrange("p (o i) -> p o i", o=1
                                                ).broadcast_to([128, 16, 16])
                for side, wb in (("q", wq_b), ("k", wk_b)):
                    for cc in range(4):
                        P.op("tensor", lambda e, tt=tt, cc=cc, wb=wb: e.matmul(
                            pQK[:], cT_s[:, 512 * cc + 128 * tt:512 * cc + 128 * (tt + 1)],
                            wb[:, CC * cc:CC * (cc + 1)],
                            start=(cc == 0), stop=(cc == 3)),
                            inc=(sPE if cc == 3 else None))
                    P.wait("vector", sPE)
                    pv = pQK[:].rearrange("p (h i two) -> p h i two", h=16, two=2)
                    t1 = pv[:, :, :, 0:1].squeeze(3)
                    t2 = pv[:, :, :, 1:2].squeeze(3)
                    r1v = rsc1[:].rearrange("p (h i) -> p h i", h=16)
                    r2v = rsc2[:].rearrange("p (h i) -> p h i", h=16)
                    qv = qr_s[:].rearrange("p (h i two) -> p h i two", h=16, two=2)
                    q1 = qv[:, :, :, 0:1].squeeze(3)
                    q2 = qv[:, :, :, 1:2].squeeze(3)
                    P.op("vector", lambda e, t1=t1, cosB=cosB, r1v=r1v: e.tensor_tensor(
                        r1v, t1, cosB, ALU.mult), inc=sVE)
                    P.op("vector", lambda e, t2=t2, sinB=sinB, r2v=r2v: e.tensor_tensor(
                        r2v, t2, sinB, ALU.mult), inc=sVE)
                    P.wait("vector", sVE)
                    P.op("vector", lambda e, q1=q1, r1v=r1v, r2v=r2v: e.tensor_sub(
                        q1, r1v, r2v), inc=sVE)
                    P.op("vector", lambda e, t1=t1, sinB=sinB, r1v=r1v: e.tensor_tensor(
                        r1v, t1, sinB, ALU.mult), inc=sVE)
                    P.op("vector", lambda e, t2=t2, cosB=cosB, r2v=r2v: e.tensor_tensor(
                        r2v, t2, cosB, ALU.mult), inc=sVE)
                    P.wait("vector", sVE)
                    P.op("vector", lambda e, q2=q2, r1v=r1v, r2v=r2v: e.tensor_add(
                        q2, r1v, r2v), inc=sVE)
                    P.wait("tensor", sVE)
                    dstT = QT_s if side == "q" else KT_s
                    for g in range(4):
                        P.op("tensor", lambda e, g=g: e.transpose(
                            pT2[:, 128 * (g % 4):128 * (g % 4 + 1)],
                            qr_s[:, 128 * g:128 * (g + 1)], eye_b[:]), inc=sPE)
                        P.wait("vector", sPE)
                        P.op("vector", lambda e, g=g, tt=tt, c=c, dstT=dstT: e.tensor_copy(
                            dstT[g][:, 512 * c + 128 * tt:512 * c + 128 * (tt + 1)],
                            pT2[:, 128 * (g % 4):128 * (g % 4 + 1)]), inc=sVE)
                        P.wait("tensor", sVE)
                for cc in range(4):
                    P.op("tensor", lambda e, tt=tt, cc=cc: e.matmul(
                        pV[:], cT_s[:, 512 * cc + 128 * tt:512 * cc + 128 * (tt + 1)],
                        wv_b[:, CC * cc:CC * (cc + 1)],
                        start=(cc == 0), stop=(cc == 3)),
                        inc=(sPE if cc == 3 else None))
                P.wait("vector", sPE)
                vsrc = pV[:].rearrange("p (h i) -> p h i", h=16)
                vdst = Vp_s[:, 528 * gt:528 * (gt + 1)].rearrange(
                    "p (h i) -> p h i", i=33)[:, :, 0:32]
                P.op("vector", lambda e, vsrc=vsrc, vdst=vdst: e.tensor_copy(
                    vdst, vsrc), inc=sVE)
                od = Vp_s[:, 528 * gt:528 * (gt + 1)].rearrange(
                    "p (h i) -> p h i", i=33)[:, :, 32:33]
                P.op("vector", lambda e, od=od: e.memset(od, 1.0), inc=sVE)
                P.wait("tensor", sVE)
            tr2_mark[c] = sPE.n
            ld2_mark[c] = sVE.n
        P.wait("sync", sX)
        P.wait("sync", sST)
        P.quiesce(sBar, [sLD, sCP, sST, sWO, sWOb, sX, sXa, sXb, sWGa, sWGb, sTQ, sW13, sW2])
        B2.close()

        if DBG and STAGE == 2:
            P.wait("gpsimd", sVE)
            P.wait("gpsimd", sSE)
            P.wait("gpsimd", sPE)
            P.dma("gpsimd", dbg["d_qt"], QT_s[0][:, 0:1024], sST)
            P.dma("gpsimd", dbg["d_kt"], KT_s[3][:, 0:1024], sST)
            P.dma("gpsimd", dbg["d_vp"], Vp_s[:, 0:1056], sST)
            P.dma("gpsimd", dbg["d_ct"], cT_s[:, 0:2048], sST)
            P.wait("gpsimd", sST)

    if STAGE >= 3:
        o_nb = mk(B, "o_nb", [128, 512], BF16)
        eyef_b = mk(B, "eyef_b", [33, 33], F32)
        P.dma("sync", eyef_b[:], eyef_c, sLD)
        P.wait("tensor", sLD)
        OT_sb = mk(B, "OT_sb", [33, 4 * 512], F32)
        nmb = mk(B, "nmb", [128, 32], BF16)
        negm_s = mk(B, "negm_s", [128, 512], BF16)
        PT_b = mk(B, "PT_b", [128, 2 * 1024], BF16)
        rq_s = mk(B, "rq_s", [128, 1], F32)
        mxt = mk(B, "mxt", [128, 8], F32)
        pMA = [mk(B, f"pMA{i}", [128, 1024], psum=True) for i in range(2)]
        pSp = mk(B, "pSp", [128, 1024], psum=True)
        pAV = [mk(B, f"pAV{i}", [128, 512], psum=True) for i in range(2)]

        units = [(g, qc) for g in range(4) for qc in range(NCH)]
        mx_rd_mark = {}
        mx_done = {}
        kt_se_done = {}
        unit_ve_done = {}

        def max_steps(u):
            g, qc = units[u]
            base = 16 * (u % 2)
            step = 0
            for qt in range(4):
                for kc in range(qc + 1):
                    if step >= 1:
                        P.wait("tensor", sVE, mx_rd_mark[(u, step - 1)])
                    elif u >= 1:
                        P.wait("tensor", sVE, mx_done[u - 1])
                        P.wait("tensor", sSE, kt_se_done.get(u - 1, 0))
                    for pr in range(2):
                        for hl in range(2):
                            hj = 2 * pr + hl
                            P.op("tensor", lambda e, g=g, qc=qc, qt=qt, kc=kc, hj=hj,
                                 pr=pr, hl=hl: e.matmul(
                                     pMA[pr][:, 512 * hl:512 * (hl + 1)],
                                     QT_s[g][32 * hj:32 * (hj + 1),
                                             512 * qc + 128 * qt:512 * qc + 128 * (qt + 1)],
                                     KT_s[g][32 * hj:32 * (hj + 1),
                                             512 * kc:512 * (kc + 1)],
                                     start=True, stop=True,
                                     tile_position=(32 * hj, 0)),
                                 inc=(sPE if (pr == 1 and hl == 1) else None))
                    mm_mark = sPE.n
                    P.wait("vector", sPE, mm_mark)
                    for pr in range(2):
                        for hl in range(2):
                            hj = 2 * pr + hl
                            col = base + 4 * hj + qt
                            srcap = pMA[pr][:, 512 * hl:512 * (hl + 1)]
                            if kc == 0:
                                P.op("vector", lambda e, col=col, srcap=srcap:
                                     e.reduce_max(nmb[:, col:col + 1], srcap, AX.X,
                                                  negate=True), inc=sVE)
                            else:
                                P.op("vector", lambda e, srcap=srcap, hj=hj:
                                     e.reduce_max(mxt[:, hj:hj + 1], srcap, AX.X,
                                                  negate=True), inc=sVE)
                                P.wait("vector", sVE)
                                P.op("vector", lambda e, col=col, hj=hj: e.tensor_tensor(
                                    nmb[:, col:col + 1], nmb[:, col:col + 1],
                                    mxt[:, hj:hj + 1], ALU.min), inc=sVE)
                    mx_rd_mark[(u, step)] = sVE.n
                    step += 1
                    yield
            mx_done[u] = sVE.n
            while True:
                yield

        def pump(gen, n=1):
            if gen is None:
                return
            for _ in range(n):
                try:
                    next(gen)
                except StopIteration:
                    pass

        def emit_unit(u, nxt_gen):
            g, qc = units[u]
            base = 16 * (u % 2)
            P.wait("tensor", sVE, mx_done[u])
            if u >= 1:
                P.wait("tensor", sVE, unit_ve_done[u - 1])
                P.wait("tensor", sSE, kt_se_done[u - 1])
            for qt in range(4):
                for hj in range(4):
                    P.op("tensor", lambda e, qt=qt, hj=hj, base=base: e.matmul(
                        pSp[32 * hj:32 * hj + 1, 128 * qt:128 * (qt + 1)],
                        nmb[:, base + 4 * hj + qt:base + 4 * hj + qt + 1], eye_b[:],
                        start=True, stop=True,
                        tile_position=(0, 32 * hj)), inc=sPE)
            P.wait("vector", sPE)
            P.op("vector", lambda e: e.tensor_copy(negm_s[:], pSp[:, 0:512]), inc=sVE)
            P.wait("tensor", sVE)
            nkt = 4 * qc + 4
            for pr in range(2):
                exp_mark = {}
                msk_mark = {}
                av_mark = {}
                for kt in range(nkt):
                    if kt >= 1:
                        P.wait("tensor", sSE, exp_mark[kt - 1])
                    for hl in range(2):
                        hj = 2 * pr + hl
                        P.op("tensor", lambda e, g=g, qc=qc, kt=kt, hj=hj, hl=hl:
                             e.matmul(
                                 pSp[:, 512 * hl:512 * (hl + 1)],
                                 KT_s[g][32 * hj:32 * (hj + 1),
                                         128 * kt:128 * (kt + 1)],
                                 QT_s[g][32 * hj:32 * (hj + 1),
                                         512 * qc:512 * (qc + 1)],
                                 start=True, stop=False,
                                 tile_position=(32 * hj, 0)), inc=None)
                        P.op("tensor", lambda e, hj=hj, hl=hl: e.matmul(
                            pSp[:, 512 * hl:512 * (hl + 1)],
                            ones_b[32 * hj:32 * hj + 1, 0:128],
                            negm_s[32 * hj:32 * hj + 1, :],
                            start=False, stop=True,
                            tile_position=(32 * hj, 0)),
                            inc=(sPE if hl == 1 else None))
                    st_mark_kt = sPE.n
                    P.wait("scalar", sPE, st_mark_kt)
                    if kt >= 2:
                        P.wait("scalar", sPE, av_mark[kt - 2])
                    P.op("scalar", lambda e, kt=kt: e.activation(
                        PT_b[:, 1024 * (kt % 2):1024 * (kt % 2 + 1)], pSp[:], AF.Exp),
                        inc=sSE)
                    exp_mark[kt] = sSE.n
                    kt_rel = kt - 4 * qc
                    if kt_rel >= 0:
                        P.wait("vector", sSE, exp_mark[kt])
                        pt3 = PT_b[:, 1024 * (kt % 2):1024 * (kt % 2 + 1)].rearrange(
                            "p (two q) -> p two q", two=2)
                        bm3 = bm_b[:, 512 * kt_rel:512 * (kt_rel + 1)].rearrange(
                            "p (o q) -> p o q", o=1).broadcast_to([128, 2, 512])
                        P.op("vector", lambda e, pt3=pt3, bm3=bm3: e.tensor_tensor(
                            pt3, pt3, bm3, ALU.mult), inc=sVE)
                        msk_mark[kt] = sVE.n
                    for hl in range(2):
                        hj = 2 * pr + hl
                        if hl == 0:
                            if kt_rel >= 0:
                                P.wait("tensor", sVE, msk_mark[kt])
                            else:
                                P.wait("tensor", sSE, exp_mark[kt])
                        P.op("tensor", lambda e, g=g, kt=kt, hj=hj, hl=hl, nkt=nkt:
                             e.matmul(
                                 pAV[hl][0:33, :],
                                 Vp_s[:, 528 * kt + 33 * (4 * g + hj):
                                      528 * kt + 33 * (4 * g + hj) + 33],
                                 PT_b[:, 1024 * (kt % 2) + 512 * hl:
                                      1024 * (kt % 2) + 512 * (hl + 1)],
                                 start=(kt == 0), stop=(kt == nkt - 1)),
                             inc=(sPE if hl == 1 else None))
                    av_mark[kt] = sPE.n
                    if kt % 2 == 0:
                        pump(nxt_gen)
                P.wait("vector", sPE, av_mark[nkt - 1])
                for hl in range(2):
                    P.op("vector", lambda e, pr=pr, hl=hl: e.tensor_copy(
                        OT_sb[:, 1024 * pr + 512 * hl:1024 * pr + 512 * (hl + 1)],
                        pAV[hl][0:33, :]), inc=sVE)
                P.wait("tensor", sVE)
            kt_se_done[u] = sSE.n
            for qt in range(4):
                for hj in range(4):
                    P.op("tensor", lambda e, hj=hj, qt=qt: e.transpose(
                        pAV[0][:, 0:33],
                        OT_sb[:, 512 * hj + 128 * qt:512 * hj + 128 * (qt + 1)],
                        eyef_b[:]), inc=sPE)
                    P.wait("vector", sPE)
                    P.op("vector", lambda e: e.reciprocal(
                        rq_s[:], pAV[0][:, 32:33]), inc=sVE)
                    P.wait("vector", sVE)
                    P.op("vector", lambda e, g=g, hj=hj: e.tensor_scalar_mul(
                        o_nb[:, 128 * g + 32 * hj:128 * g + 32 * (hj + 1)],
                        pAV[0][:, 0:32], rq_s[:]), inc=sVE)
                    P.wait("tensor", sVE)
                P.wait("sync", sVE)
                qi = 4 * qc + qt
                P.dma("sync", o_d[128 * qi:128 * (qi + 1),
                                  128 * g:128 * (g + 1)],
                      o_nb[:, 128 * g:128 * (g + 1)], sST)
                P.wait("vector", sST)
                pump(nxt_gen)
            unit_ve_done[u] = sVE.n

        g0 = max_steps(0)
        pump(g0, 200)
        for u in range(len(units)):
            gn = max_steps(u + 1) if u + 1 < len(units) else None
            emit_unit(u, gn)
            pump(gn, 200)
        P.wait("sync", sST)
        if DBG and STAGE == 3:
            P.wait("gpsimd", sVE)
            P.wait("gpsimd", sSE)
            P.wait("gpsimd", sPE)
            P.wait("gpsimd", sST)
            P.dma("gpsimd", dbg["d_o"], o_d[0:128, :], sST)
            P.dma("gpsimd", dbg["d_tt"][:, 0:512], o_d[S - 128:S, 0:512], sST)
            P.wait("gpsimd", sST)
    if STAGE >= 2:
        P.quiesce(sBar, [sLD, sCP, sST, sWO, sWOb, sX, sXa, sXb, sWGa, sWGb, sTQ, sW13, sW2])
        B.close()

    if STAGE >= 4:
        C = ExitStack()
        oc_s = mk(C, "oc_s", [128, 2048], BF16)
        oT_s = mk(C, "oT_s", [128, 2048], BF16)
        x1b4 = [mk(C, f"x1b4{i}", [128, 4096], F32) for i in range(2)]
        xf4 = mk(C, "xf4", [128, 1024], F32)
        x2_t = mk(C, "x2_t", [128, 1024], F32)
        t_s = mk(C, "t_s", [128, 4096], BF16)
        tT_sb = mk(C, "tT_sb", [128, 4096], BF16)
        ge_s = mk(C, "ge_s", [128, 8], F32)
        t8 = mk(C, "t8", [128, 40], F32)
        pUP = [mk(C, f"pUP{i}", [128, 512], psum=True) for i in range(2)]
        pT4 = mk(C, "pT4", [128, 512], BF16, psum=True)
        pRT = mk(C, "pRT", [128, 8], psum=True)
        ld4_mark = {}
        p4pe_mark = {}
        for c in range(NCH):
            xb = x1b4[c % 2]
            if c >= 2:
                P.wait("sync", sVE, ld4_mark[c - 2])
            if c >= 1:
                P.wait("sync", sPE, p4pe_mark[c - 1])
            xsem = sXa if c % 2 == 0 else sXb
            for tt in range(4):
                P.dma("sync", xb[:, 1024 * tt:1024 * (tt + 1)],
                      x1_d[512 * c + 128 * tt:512 * c + 128 * (tt + 1), :], xsem)
            for tt in range(4):
                P.dma("sync", oc_s[:, 512 * tt:512 * (tt + 1)],
                      o_d[512 * c + 128 * tt:512 * c + 128 * (tt + 1), :], xsem)
            P.wait("tensor", xsem)
            P.wait("vector", xsem)
            P.wait("scalar", xsem)
            for tt in range(4):
                for j in range(4):
                    P.op("tensor", lambda e, tt=tt, j=j: e.transpose(
                        pT4[:, 128 * (j % 4):128 * (j % 4 + 1)],
                        oc_s[:, 512 * tt + 128 * j:512 * tt + 128 * (j + 1)],
                        eye_b[:]), inc=sPE)
                    P.wait("vector", sPE)
                    P.op("vector", lambda e, tt=tt, j=j: e.tensor_copy(
                        oT_s[:, 512 * j + 128 * tt:512 * j + 128 * (tt + 1)],
                        pT4[:, 128 * (j % 4):128 * (j % 4 + 1)]), inc=sVE)
                    P.wait("tensor", sVE)
            for tt in range(4):
                gt = 4 * c + tt
                xt = xb[:, 1024 * tt:1024 * (tt + 1)]
                ln_stats(xt, 0, xf4[:])
                P.wait("scalar", sVE)
                P.op("scalar", lambda e, xt=xt: e.activation(
                    xf4[:], xt, AF.Identity, bias=st1[:, 0:1], scale=st1[:, 1:2]),
                    inc=sSE)
                P.wait("tensor", sSE)
                for half in range(2):
                    for cc in range(4):
                        P.op("tensor", lambda e, tt=tt, cc=cc, half=half: e.matmul(
                            pUP[half][:],
                            oT_s[:, 512 * cc + 128 * tt:512 * cc + 128 * (tt + 1)],
                            wup_b[:, D * cc + 512 * half:D * cc + 512 * (half + 1)],
                            start=(cc == 0), stop=(cc == 3)),
                            inc=(sPE if cc == 3 else None))
                P.wait("vector", sPE)
                for half in range(2):
                    P.op("vector", lambda e, half=half: e.tensor_add(
                        x2_t[:, 512 * half:512 * (half + 1)], pUP[half][:],
                        xf4[:, 512 * half:512 * (half + 1)]), inc=sVE)
                P.wait("tensor", sVE)
                P.wait("sync", sVE)
                P.dma("sync", res_d[512 * c + 128 * tt:512 * c + 128 * (tt + 1), :],
                      x2_t[:], sST)
                P.op("vector", lambda e: e.scalar_tensor_tensor(
                    sc1[:], x2_t[:], 1.0, x2_t[:], ALU.mult, ALU.mult,
                    accum_out=st2[:, 0:1]), inc=sVE)
                P.wait("vector", sVE)
                P.op("vector", lambda e: e.tensor_scalar_mul(
                    st2[:, 0:1], st2[:, 0:1], 1.0 / D), inc=sVE)
                P.wait("scalar", sVE)
                P.op("scalar", lambda e: e.activation(
                    st2[:, 1:2], st2[:, 0:1], AF.Sqrt, bias=epsb[:]), inc=sSE)
                P.wait("vector", sSE)
                P.op("vector", lambda e: e.reciprocal(st2[:, 0:1], st2[:, 1:2]),
                     inc=sVE)
                P.wait("vector", sVE)
                P.op("vector", lambda e, tt=tt: e.scalar_tensor_tensor(
                    t_s[:, 1024 * tt:1024 * (tt + 1)], x2_t[:], st2[:, 0:1],
                    g2_b[:], ALU.mult, ALU.mult), inc=sVE)
            P.wait("tensor", sVE)
            for tt in range(4):
                for j in range(8):
                    P.op("tensor", lambda e, tt=tt, j=j: e.transpose(
                        pT4[:, 128 * (j % 4):128 * (j % 4 + 1)],
                        t_s[:, 1024 * tt + 128 * j:1024 * tt + 128 * (j + 1)],
                        eye_b[:]), inc=sPE)
                    P.wait("vector", sPE)
                    P.op("vector", lambda e, tt=tt, j=j: e.tensor_copy(
                        tT_sb[:, 512 * j + 128 * tt:512 * j + 128 * (tt + 1)],
                        pT4[:, 128 * (j % 4):128 * (j % 4 + 1)]), inc=sVE)
                    P.wait("tensor", sVE)
            P.wait("sync", sVE)
            for dc in range(8):
                P.dma("sync", tT_d[128 * dc:128 * (dc + 1), 512 * c:512 * (c + 1)],
                      tT_sb[:, 512 * dc:512 * (dc + 1)], sST)
            for tt in range(4):
                gt = 4 * c + tt
                for dc in range(8):
                    P.op("tensor", lambda e, tt=tt, dc=dc: e.matmul(
                        pRT[:], tT_sb[:, 512 * dc + 128 * tt:512 * dc + 128 * (tt + 1)],
                        rtr_b[:, E * dc:E * (dc + 1)],
                        start=(dc == 0), stop=(dc == 7)),
                        inc=(sPE if dc == 7 else None))
                P.wait("vector", sPE)
                P.op("vector", lambda e: e.reduce_max(
                    t8[:, 0:1], pRT[:], AX.X, negate=True), inc=sVE)
                P.wait("scalar", sVE)
                P.op("scalar", lambda e: e.activation(
                    ge_s[:], pRT[:], AF.Exp, bias=t8[:, 0:1]), inc=sSE)
                P.wait("tensor", sSE)
                P.wait("vector", sSE)
                P.op("vector", lambda e: e.tensor_scalar(
                    t8[:, 8:16], ge_s[:], 1.0, None, ALU.is_equal), inc=sVE)
                P.wait("vector", sVE)
                P.op("vector", lambda e: e.tensor_mul(
                    t8[:, 8:16], t8[:, 8:16], wcol_b[:]), inc=sVE)
                P.wait("vector", sVE)
                P.op("vector", lambda e: e.reduce_max(t8[:, 1:2], t8[:, 8:16], AX.X),
                     inc=sVE)
                P.wait("vector", sVE)
                P.op("vector", lambda e: e.tensor_scalar(
                    t8[:, 8:16], t8[:, 8:16], t8[:, 1:2], None, ALU.is_equal),
                    inc=sVE)
                P.wait("vector", sVE)
                P.op("vector", lambda e: e.scalar_tensor_tensor(
                    t8[:, 16:24], t8[:, 8:16], -1e30, ge_s[:], ALU.mult, ALU.add),
                    inc=sVE)
                P.wait("vector", sVE)
                P.op("vector", lambda e: e.reduce_max(t8[:, 2:3], t8[:, 16:24], AX.X),
                     inc=sVE)
                P.wait("vector", sVE)
                P.op("vector", lambda e: e.tensor_scalar(
                    t8[:, 24:32], t8[:, 16:24], t8[:, 2:3], None, ALU.is_equal),
                    inc=sVE)
                P.wait("vector", sVE)
                P.op("vector", lambda e: e.tensor_mul(
                    t8[:, 24:32], t8[:, 24:32], wcol_b[:]), inc=sVE)
                P.wait("vector", sVE)
                P.op("vector", lambda e: e.reduce_max(t8[:, 3:4], t8[:, 24:32], AX.X),
                     inc=sVE)
                P.wait("vector", sVE)
                P.op("vector", lambda e: e.tensor_scalar(
                    t8[:, 24:32], t8[:, 24:32], t8[:, 3:4], None, ALU.is_equal),
                    inc=sVE)
                P.wait("vector", sVE)
                P.op("vector", lambda e: e.tensor_add(
                    t8[:, 8:16], t8[:, 8:16], t8[:, 24:32]), inc=sVE)
                P.op("vector", lambda e: e.tensor_scalar(
                    t8[:, 4:5], t8[:, 2:3], 1.0, None, ALU.add), inc=sVE)
                P.wait("vector", sVE)
                P.op("vector", lambda e: e.reciprocal(t8[:, 5:6], t8[:, 4:5]), inc=sVE)
                P.wait("vector", sVE)
                P.op("vector", lambda e: e.tensor_mul(
                    t8[:, 8:16], t8[:, 8:16], ge_s[:]), inc=sVE)
                P.wait("vector", sVE)
                P.op("vector", lambda e, gt=gt: e.tensor_scalar_mul(
                    gate_b[:, E * gt:E * (gt + 1)], t8[:, 8:16], t8[:, 5:6]), inc=sVE)
            ld4_mark[c] = sVE.n
            p4pe_mark[c] = sPE.n
        P.wait("sync", sST)
        P.quiesce(sBar, [sLD, sCP, sST, sWO, sWOb, sX, sXa, sXb, sWGa, sWGb, sTQ, sW13, sW2])
        C.close()
        MID2.close()
        if DBG and STAGE == 4:
            P.dma("gpsimd", dbg["d_gate"], gate_b[:, 0:256], sST)
            P.dma("gpsimd", dbg["d_tt"], tT_d[0:128, 0:1024], sST)
            P.dma("gpsimd", dbg["d_res"], res_d[0:128, :], sST)
            P.wait("gpsimd", sST)

    if globals().get('DEVDUMP') and STAGE >= 4:
        P.wait("gpsimd", sVE)
        P.wait("gpsimd", sPE)
        P.wait("gpsimd", sSE)
        P.wait("gpsimd", sST)
        for rb in range(8):
            P.dma("gpsimd", dbg["oo"][512 * rb:512 * (rb + 1), :],
                  o_d[512 * rb:512 * (rb + 1), :], sST)
            P.dma("gpsimd", dbg["tto"][128 * rb:128 * (rb + 1), :],
                  tT_d[128 * rb:128 * (rb + 1), :], sST)
            P.dma("gpsimd", dbg["reso"][512 * rb:512 * (rb + 1), :],
                  res_d[512 * rb:512 * (rb + 1), :], sST)
        P.dma("gpsimd", dbg["gto"], gate_b[:], sST)
        P.wait("gpsimd", sST)

    if STAGE >= 5:
        Dx = ExitStack()
        w1e_s = mk(Dx, "w1e_s", [128, 8 * MF], BF16)
        w3e_s = mk(Dx, "w3e_s", [128, 8 * MF], BF16)
        w2e_s = mk(Dx, "w2e_s", [128, 16 * D], BF16)
        hidT_s = mk(Dx, "hidT_s", [128, 16 * 1024], BF16)
        acc_s = mk(Dx, "acc_s", [128, 8 * 1024], F32)
        tTq_s = mk(Dx, "tTq_s", [128, 8 * 1024], BF16)
        res_t = mk(Dx, "res_t", [128, 1024], F32)
        x1_t5 = mk(Dx, "x1_t5", [128, 1024], F32)
        f_t5 = mk(Dx, "f_t5", [128, 1024], F32)
        sil_s = mk(Dx, "sil_s", [128, 1024], BF16)
        pH = [mk(Dx, f"pH{i}", [128, 512], psum=True) for i in range(4)]
        pY = [mk(Dx, f"pY{i}", [128, 512], psum=True) for i in range(4)]
        wmk = {}
        w13_mark = {}
        w2_mark = {}

        def load_w13(e_, q4_):
            for dc in range(8):
                P.dma("gpsimd", w1e_s[:, MF * dc:MF * (dc + 1)],
                      w1[e_, 128 * dc:128 * (dc + 1), :], sW13)
                P.dma("gpsimd", w3e_s[:, MF * dc:MF * (dc + 1)],
                      w3[e_, 128 * dc:128 * (dc + 1), :], sW13)

        def load_w2(e_):
            for fc in range(16):
                P.dma("gpsimd", w2e_s[:, D * fc:D * (fc + 1)],
                      w2[e_, 128 * fc:128 * (fc + 1), :], sW2)

        load_w13(0, 0)
        w13_mark[(0, 0)] = sW13.n
        load_w2(0)
        w2_mark[(0, 0)] = sW2.n
        for q4 in range(4):
            if q4 >= 1:
                P.wait("sync", sPE, wmk["hid_end"])
            for dc in range(8):
                P.dma("sync", tTq_s[:, 1024 * dc:1024 * (dc + 1)],
                      tT_d[128 * dc:128 * (dc + 1), 1024 * q4:1024 * (q4 + 1)], sTQ)
            for e_ in range(E):
                P.wait("tensor", sW13, w13_mark[(q4, e_)])
                P.wait("tensor", sTQ)
                for ft in range(16):
                    for half in range(2):
                        if ft >= 1:
                            P.wait("tensor", sVE, wmk[("hid_ev", half)])
                        for dc in range(8):
                            P.op("tensor", lambda e, ft=ft, dc=dc, half=half: e.matmul(
                                pH[half][:],
                                w1e_s[:, MF * dc + 128 * ft:MF * dc + 128 * (ft + 1)],
                                tTq_s[:, 1024 * dc + 512 * half:1024 * dc + 512 * (half + 1)],
                                start=(dc == 0), stop=(dc == 7)), inc=None)
                        for dc in range(8):
                            P.op("tensor", lambda e, ft=ft, dc=dc, half=half: e.matmul(
                                pH[2 + half][:],
                                w3e_s[:, MF * dc + 128 * ft:MF * dc + 128 * (ft + 1)],
                                tTq_s[:, 1024 * dc + 512 * half:1024 * dc + 512 * (half + 1)],
                                start=(dc == 0), stop=(dc == 7)),
                                inc=(sPE if dc == 7 else None))
                        wmk[("hid_mm", half)] = sPE.n
                    for half in range(2):
                        P.wait("scalar", sPE, wmk[("hid_mm", half)])
                        if ft >= 1:
                            P.wait("scalar", sVE, wmk[("hid_ev", half)])
                        P.op("scalar", lambda e, half=half: e.activation(
                            sil_s[:, 512 * half:512 * (half + 1)], pH[half][:],
                            AF.Silu), inc=sSE)
                        wmk[("sil", half)] = sSE.n
                        P.wait("vector", sSE, wmk[("sil", half)])
                        P.op("vector", lambda e, ft=ft, half=half: e.tensor_tensor(
                            hidT_s[:, 1024 * ft + 512 * half:1024 * ft + 512 * (half + 1)],
                            sil_s[:, 512 * half:512 * (half + 1)],
                            pH[2 + half][:], ALU.mult), inc=sVE)
                        wmk[("hid_ev", half)] = sVE.n
                hid_mm_end = sPE.n
                wmk["hid_end"] = hid_mm_end
                P.wait("gpsimd", sPE, hid_mm_end)
                nxt = (q4, e_ + 1) if e_ + 1 < E else (q4 + 1, 0)
                if nxt[0] < 4:
                    load_w13(nxt[1], nxt[0])
                    w13_mark[nxt] = sW13.n
                P.wait("tensor", sVE)
                P.wait("tensor", sW2, w2_mark[(q4, e_)])
                for tt in range(8):
                    pyb = [pY[2 * (tt % 2)], pY[2 * (tt % 2) + 1]]
                    if tt >= 2:
                        P.wait("tensor", sVE, wmk[("y_ev", tt % 2)])
                    for ft in range(16):
                        for half in range(2):
                            P.op("tensor", lambda e, tt=tt, ft=ft, half=half, pyb=pyb:
                                 e.matmul(
                                     pyb[half][:],
                                     hidT_s[:, 1024 * ft + 128 * tt:1024 * ft + 128 * (tt + 1)],
                                     w2e_s[:, D * ft + 512 * half:D * ft + 512 * (half + 1)],
                                     start=(ft == 0), stop=(ft == 15)),
                                 inc=(sPE if (ft == 15 and half == 1) else None))
                    P.wait("vector", sPE)
                    gt = 8 * q4 + tt
                    for half in range(2):
                        dst = acc_s[:, 1024 * tt + 512 * half:1024 * tt + 512 * (half + 1)]
                        if e_ == 0:
                            P.op("vector", lambda e, dst=dst, half=half, gt=gt, pyb=pyb:
                                 e.tensor_scalar(
                                     dst, pyb[half][:],
                                     gate_b[:, E * gt + 0:E * gt + 1], None, ALU.mult),
                                 inc=sVE)
                        else:
                            P.op("vector", lambda e, dst=dst, half=half, gt=gt, e_=e_,
                                 pyb=pyb: e.scalar_tensor_tensor(
                                     dst, pyb[half][:],
                                     gate_b[:, E * gt + e_:E * gt + e_ + 1], dst,
                                     ALU.mult, ALU.add), inc=sVE)
                    wmk[("y_ev", tt % 2)] = sVE.n
                ymm_end = sPE.n
                P.wait("gpsimd", sPE, ymm_end)
                if nxt[0] < 4:
                    load_w2(nxt[1])
                    w2_mark[nxt] = sW2.n
            for tt in range(8):
                gq = 8 * q4 + tt
                P.wait("sync", sVE)
                P.dma("sync", res_t[:], res_d[128 * gq:128 * (gq + 1), :], sX)
                P.dma("sync", x1_t5[:], x1_d[128 * gq:128 * (gq + 1), :], sX)
                P.wait("vector", sX)
                P.op("vector", lambda e, tt=tt: e.tensor_add(
                    f_t5[:], acc_s[:, 1024 * tt:1024 * (tt + 1)], res_t[:]), inc=sVE)
                P.wait("vector", sVE)
                ln_stats(f_t5[:], 0, sc1[:])
                P.wait("scalar", sVE)
                P.op("scalar", lambda e: e.activation(
                    sc2[:], f_t5[:], AF.Identity, bias=st1[:, 0:1],
                    scale=st1[:, 1:2]), inc=sSE)
                P.wait("vector", sSE)
                P.op("vector", lambda e: e.tensor_add(f_t5[:], sc2[:], x1_t5[:]),
                     inc=sVE)
                P.wait("sync", sVE)
                P.dma("sync", out_t[128 * gq:128 * (gq + 1), :], f_t5[:], sST)
                P.wait("vector", sST)
                P.wait("sync", sST)
        P.wait("sync", sST)
        P.wait("sync", sX)
        P.wait("sync", sTQ)
        P.wait("sync", sW13)
        P.wait("sync", sW2)
        P.barrier(sBar)
        Dx.close()

    P.wait("sync", sST)
    P.wait("sync", sLD)
    P.wait("sync", sCP)
    P.barrier(sBar)
    if STAGE < 4:
        MID2.close()
    with nc.Block() as block:
        P.build(block)
    MID.close()
    G.close()
    return nc


def kernel(**inputs):
    xfull = np.ascontiguousarray(inputs["L0_attn_x"])
    nc = build_nc()
    common = {
        "W_in": inputs["L0_attn_W_in"], "b_in": inputs["L0_attn_b_in"],
        "b_sp": inputs["L0_attn_b_sp"], "W_g": inputs["L0_attn_W_g"],
        "b_g": inputs["L0_attn_b_g"], "W_out": inputs["L0_attn_W_out"],
        "b_out": inputs["L0_attn_b_out"], "norm_w": inputs["L0_attn_norm_w"],
        "norm_b": inputs["L0_attn_norm_b"], "g1": inputs["L0_ffn_g1"],
        "g2": inputs["L0_ffn_g2"], "w_down": inputs["L0_ffn_w_down"],
        "wq": inputs["L0_ffn_wq"], "wk": inputs["L0_ffn_wk"],
        "wv": inputs["L0_ffn_wv"], "w_up": inputs["L0_ffn_w_up"],
        "router_w": inputs["L0_ffn_router_w"], "w1": inputs["L0_ffn_w1"],
        "w2": inputs["L0_ffn_w2"], "w3": inputs["L0_ffn_w3"],
    }
    common = {k: np.ascontiguousarray(v) for k, v in common.items()}
    in_maps = [dict(common, x=np.ascontiguousarray(xfull[b])) for b in range(8)]
    res = run_bass_kernel_spmd(nc, in_maps, core_ids=list(range(8)))
    global LAST
    LAST = res
    return np.stack([res.results[b]["out"] for b in range(8)], axis=0)


# version 06 (final)
# speedup vs baseline: 1.0607x; 1.0607x over previous
import sys
import numpy as np
import ml_dtypes
from contextlib import ExitStack

import concourse.bass as bass
import concourse.mybir as mybir
from concourse.bass_utils import run_bass_kernel_spmd

F32 = mybir.dt.float32
BF16 = mybir.dt.bfloat16
AF = mybir.ActivationFunctionType
ALU = mybir.AluOpType
AX = mybir.AxisListType

S = 4096
D = 1024
FF = 2048
CC = 512
H = 16
HD = 32
E = 8
MF = 2048
NT = S // 128
NCH = S // 512
EPS = 1e-5
RSQRT_HD = float(1.0 / np.sqrt(32.0))
PI = float(np.pi)

STAGE = 5
DBG = False
_IGNORE = 0

bf = ml_dtypes.bfloat16


class Sem:
    def __init__(self, handle, name):
        self.h = handle
        self.name = name
        self.n = 0


class Prog:

    ENGINES = ("sync", "gpsimd", "tensor", "vector", "scalar")

    def __init__(self, nc):
        self.nc = nc
        self.ops = {e: [] for e in self.ENGINES}

    def sem(self, ctx, name):
        return Sem(ctx.enter_context(self.nc.semaphore(name)), name)

    def emit(self, engine, fn):
        self.ops[engine].append(fn)

    def wait(self, engine, sem, val=None):
        v = sem.n if val is None else val
        ln = sys._getframe(1).f_lineno
        if v > 0:
            self.emit(engine, lambda e, s=sem, v=v, ln=ln:
                      e.wait_ge(s.h, v).annotate(f"w{ln}"))

    def op(self, engine, fn, inc=None, inc_by=1):
        if inc is not None:
            inc.n += inc_by
            self.emit(engine, lambda e, f=fn, s=inc, k=inc_by: f(e).then_inc(s.h, k))
        else:
            self.emit(engine, fn)

    def dma(self, engine, out, in_, sem, **kw):
        sem.n += 16
        self.emit(engine, lambda e, o=out, i=in_, s=sem, kw=kw:
                  e.dma_start(o, i, **kw).then_inc(s.h, 16))

    def quiesce(self, bar, dma_sems):
        for e in self.ENGINES:
            for s in dma_sems:
                self.wait(e, s)
        self.barrier(bar)

    def barrier(self, bar):
        target = bar.n + len(self.ENGINES)
        for e in self.ENGINES:
            self.emit(e, lambda en, s=bar: en.nop().then_inc(s.h, 1))
        bar.n = target
        for e in self.ENGINES:
            self.wait(e, bar)

    def build(self, block):
        for name in self.ENGINES:
            ops = self.ops[name]

            def make(ops):
                def body(eng):
                    for f in ops:
                        f(eng)
                return body
            getattr(block, name)(make(ops))


def build_nc():
    nc = bass.Bass("TRN2", target_bir_lowering=False)

    def din(name, shape, dt=F32):
        return nc.dram_tensor(name, list(shape), dt, kind="ExternalInput").ap()

    x_in = din("x", (S, D))
    W_in = din("W_in", (D, FF))
    b_in = din("b_in", (FF,))
    b_sp = din("b_sp", (FF,))
    W_g = din("W_g", (FF, FF))
    b_g = din("b_g", (FF,))
    W_out = din("W_out", (FF, D))
    b_out = din("b_out", (D,))
    norm_w = din("norm_w", (D,))
    norm_b = din("norm_b", (D,))
    g1 = din("g1", (D,))
    g2 = din("g2", (D,))
    w_down = din("w_down", (D, CC))
    wq = din("wq", (CC, CC))
    wk = din("wk", (CC, CC))
    wv = din("wv", (CC, CC))
    w_up = din("w_up", (CC, D))
    router_w = din("router_w", (D, E))
    w1 = din("w1", (E, D, MF))
    w2 = din("w2", (E, MF, D))
    w3 = din("w3", (E, D, MF))

    out_t = nc.dram_tensor("out", [S, D], F32, kind="ExternalOutput").ap()
    dbg = {}
    if globals().get('DEVDUMP'):
        dbg["x1o"] = nc.dram_tensor("x1o", [S, D], F32, kind="ExternalOutput").ap()
        dbg["oo"] = nc.dram_tensor("oo", [S, CC], BF16, kind="ExternalOutput").ap()
        dbg["tto"] = nc.dram_tensor("tto", [D, S], BF16, kind="ExternalOutput").ap()
        dbg["reso"] = nc.dram_tensor("reso", [S, D], F32, kind="ExternalOutput").ap()
        dbg["cwo"] = nc.dram_tensor("cwo", [128, 8 * FF], BF16, kind="ExternalOutput").ap()
        dbg["woo"] = nc.dram_tensor("woo", [128, 16 * D], BF16, kind="ExternalOutput").ap()
        dbg["gto"] = nc.dram_tensor("gto", [128, NT * E], F32, kind="ExternalOutput").ap()
    if DBG:
        for nm, sh in [("d_cw", [128, FF]), ("d_wo", [128, D]), ("d_bcT", [128, 16]),
                       ("d_st1", [128, 64]), ("d_xnb", [128, 1024]),
                       ("d_gt", [128, 512]), ("d_x0", [128, 1024]),
                       ("d_bspf", [128, 16]), ("d_bsml", [128, 32]),
                       ("d_bgf", [128, 16]), ("d_winT", [128, 1024]),
                       ("d_qt", [128, 1024]), ("d_kt", [128, 1024]),
                       ("d_vp", [128, 1056]), ("d_ct", [128, 2048]),
                       ("d_o", [128, 512]), ("d_gate", [128, 256]),
                       ("d_tt", [128, 1024]), ("d_res", [128, 1024])]:
            dbg[nm] = nc.dram_tensor(nm, sh, F32, kind="ExternalOutput").ap()

    x1_d = nc.dram_tensor("x1_d", [S, D], F32, kind="Internal").ap()
    res_d = nc.dram_tensor("res_d", [S, D], F32, kind="Internal").ap()
    o_d = nc.dram_tensor("o_d", [S, CC], BF16, kind="Internal").ap()
    tT_d = nc.dram_tensor("tT_d", [D, S], BF16, kind="Internal").ap()

    eye_c = nc.inline_tensor(np.eye(128, dtype=bf), "eye_c").ap()
    bm_np = np.zeros((4, 128, 512), dtype=bf)
    kk = np.arange(128)[:, None]
    qq = np.arange(512)[None, :]
    for j in range(4):
        bm_np[j] = ((128 * j + kk) <= qq).astype(bf)
    bm_c = nc.inline_tensor(bm_np.reshape(4 * 128, 512), "bm_c").ap()
    inv_freq_np = (1.0 / 10000.0 ** (np.arange(0, HD, 2, dtype=np.float32) / HD)
                   ).astype(np.float32)
    invf_c = nc.inline_tensor(np.broadcast_to(inv_freq_np, (128, 16)).copy(), "invf_c").ap()
    wcol_c = nc.inline_tensor(
        np.broadcast_to(8.0 - np.arange(8, dtype=np.float32), (128, 8)).copy(),
        "wcol_c").ap()
    eyef_c = nc.inline_tensor(np.eye(33, dtype=np.float32), "eyef_c").ap()

    G = ExitStack()
    MID = ExitStack()
    P = Prog(nc)
    sLD = P.sem(G, "sLD")
    sCP = P.sem(G, "sCP")
    sPE = P.sem(G, "sPE")
    sVE = P.sem(G, "sVE")
    sSE = P.sem(G, "sSE")
    sST = P.sem(G, "sST")
    sBar = P.sem(G, "sBar")
    sX = P.sem(G, "sX")
    sWO = P.sem(G, "sWO")
    sXa = P.sem(G, "sXa")
    sXb = P.sem(G, "sXb")
    sWGa = P.sem(G, "sWGa")
    sWGb = P.sem(G, "sWGb")
    sWOb = P.sem(G, "sWOb")
    sTQ = P.sem(G, "sTQ")
    sW13 = P.sem(G, "sW13")
    sW2 = P.sem(G, "sW2")
    sRS = P.sem(G, "sRS")
    sTS = P.sem(G, "sTS")
    sW = P.sem(G, "sW")

    def mk(ctx_, name, shape, dt=F32, psum=False):
        f = nc.psum_tensor if psum else nc.sbuf_tensor
        return ctx_.enter_context(f(name, list(shape), dt))

    eye_b = mk(G, "eye_b", [128, 128], BF16)
    bm_b = mk(G, "bm_b", [128, 4 * 512], BF16)
    invf_b = mk(G, "invf_b", [128, 16], F32)
    wcol_b = mk(G, "wcol_b", [128, 8], F32)
    iotaP = mk(G, "iotaP", [128, 1], F32)
    epsb = mk(G, "epsb", [128, 1], F32)
    gate_b = mk(G, "gate_b", [128, NT * E], F32)

    P.dma("sync", eye_b[:], eye_c, sLD)
    for j in range(4):
        P.dma("sync", bm_b[:, 512 * j:512 * (j + 1)], bm_c[128 * j:128 * (j + 1), :], sLD)
    P.dma("sync", invf_b[:], invf_c, sLD)
    P.dma("sync", wcol_b[:], wcol_c, sLD)
    P.op("gpsimd", lambda e: e.iota(iotaP[:], pattern=[[0, 1]], base=0,
                                    channel_multiplier=1,
                                    allow_small_or_imprecise_dtypes=True), inc=sCP)
    P.op("gpsimd", lambda e: e.memset(epsb[:], EPS), inc=sCP)

    st1 = mk(MID, "st1", [128, 64], F32)
    sc1 = mk(MID, "sc1", [128, 1024], F32)
    sc2 = mk(MID, "sc2", [128, 1024], F32)
    MID2 = ExitStack()
    g1_b = mk(MID2, "g1_b", [128, D], F32)
    g2_b = mk(MID2, "g2_b", [128, D], F32)
    wdn_b = mk(MID2, "wdn_b", [128, 8 * CC], BF16)
    wq_b = mk(MID2, "wq_b", [128, 4 * CC], BF16)
    wk_b = mk(MID2, "wk_b", [128, 4 * CC], BF16)
    wv_b = mk(MID2, "wv_b", [128, 4 * CC], BF16)
    wup_b = mk(MID2, "wup_b", [128, 4 * D], BF16)
    rtr_b = mk(MID2, "rtr_b", [128, 8 * E], BF16)

    for name, dst, src in [("g1", g1_b, g1), ("g2", g2_b, g2)]:
        P.dma("sync", dst[0:1, :], src.rearrange("(o d) -> o d", o=1), sLD)
    for dst in (g1_b, g2_b):
        rows = 1
        while rows < 128:
            P.wait("sync", sLD)
            P.wait("sync", sST)
            P.dma("sync", dst[rows:2 * rows, :], dst[0:rows, :], sST)
            rows *= 2
    for dc in range(8):
        P.dma("gpsimd", wdn_b[:, CC * dc:CC * (dc + 1)],
              w_down[128 * dc:128 * (dc + 1), :], sCP)
    for cc in range(4):
        P.dma("gpsimd", wq_b[:, CC * cc:CC * (cc + 1)], wq[128 * cc:128 * (cc + 1), :], sCP)
        P.dma("gpsimd", wk_b[:, CC * cc:CC * (cc + 1)], wk[128 * cc:128 * (cc + 1), :], sCP)
        P.dma("gpsimd", wv_b[:, CC * cc:CC * (cc + 1)], wv[128 * cc:128 * (cc + 1), :], sCP)
        P.dma("gpsimd", wup_b[:, D * cc:D * (cc + 1)], w_up[128 * cc:128 * (cc + 1), :], sCP)
    for dc in range(8):
        P.dma("gpsimd", rtr_b[:, E * dc:E * (dc + 1)],
              router_w[128 * dc:128 * (dc + 1), :], sCP)
    P.wait("vector", sCP)
    for cc in range(4):
        P.op("vector", lambda e, cc=cc: e.tensor_scalar_mul(
            wq_b[:, CC * cc:CC * (cc + 1)], wq_b[:, CC * cc:CC * (cc + 1)], RSQRT_HD),
            inc=sVE)

    A = ExitStack()
    cw_b = mk(A, "cw_b", [128, 8 * FF], BF16)
    wo_b = mk(A, "wo_b", [128, 16 * D], BF16)
    bcT_b = mk(A, "bcT_b", [128, 16], F32)
    bout_b = mk(A, "bout_b", [128, D], F32)
    nw_b = mk(A, "nw_b", [128, D], F32)
    nb_b = mk(A, "nb_b", [128, D], F32)
    for name, dst, src_ in [("bout", bout_b, b_out), ("nw", nw_b, norm_w),
                            ("nb", nb_b, norm_b)]:
        P.dma("sync", dst[0:1, :], src_.rearrange("(o d) -> o d", o=1), sLD)
    for dst in (bout_b, nw_b, nb_b):
        rows = 1
        while rows < 128:
            P.wait("sync", sLD)
            P.wait("sync", sST)
            P.dma("sync", dst[rows:2 * rows, :], dst[0:rows, :], sST)
            rows *= 2

    with nc.sbuf_tensor("winT", [128, 16 * D], BF16) as winT, \
         nc.sbuf_tensor("bsml", [128, 32], BF16) as bsml, \
         nc.sbuf_tensor("bgf", [128, 16], F32) as bgf, \
         nc.sbuf_tensor("bspf", [128, 16], F32) as bspf:

        for c16 in range(16):
            P.dma("gpsimd", bsml[:, c16:c16 + 1],
                  b_in[128 * c16:128 * (c16 + 1)].rearrange("(p o) -> p o", o=1), sCP)
            P.dma("gpsimd", bsml[:, 16 + c16:16 + c16 + 1],
                  b_sp[128 * c16:128 * (c16 + 1)].rearrange("(p o) -> p o", o=1), sCP)
            P.dma("sync", bgf[:, c16:c16 + 1],
                  b_g[128 * c16:128 * (c16 + 1)].rearrange("(p o) -> p o", o=1), sLD)
            P.dma("sync", bspf[:, c16:c16 + 1],
                  b_sp[128 * c16:128 * (c16 + 1)].rearrange("(p o) -> p o", o=1), sLD)

        SKIP_WC = bool(globals().get('SKIP_WC'))
        with nc.sbuf_tensor("winb", [128, 2 * FF], BF16) as winb, \
             nc.psum_tensor("pp0", [128, 512], BF16) as pp0:
            for dc in range(8 if not SKIP_WC else 0):
                if dc >= 2:
                    P.wait("gpsimd", sVE, tr_evac_mark[dc - 2])
                wsem = sWGa if dc % 2 == 0 else sWGb
                P.dma("gpsimd", winb[:, FF * (dc % 2):FF * (dc % 2 + 1)],
                      W_in[128 * dc:128 * (dc + 1), :], wsem)
                P.wait("tensor", wsem)
                for j in range(16):
                    P.op("tensor", lambda e, dc=dc, j=j: e.transpose(
                        pp0[:, 128 * (j % 4):128 * (j % 4 + 1)],
                        winb[:, FF * (dc % 2) + 128 * j:FF * (dc % 2) + 128 * (j + 1)],
                        eye_b[:]), inc=sPE)
                    P.wait("vector", sPE)
                    P.op("vector", lambda e, dc=dc, j=j: e.tensor_copy(
                        winT[:, D * j + 128 * dc:D * j + 128 * (dc + 1)],
                        pp0[:, 128 * (j % 4):128 * (j % 4 + 1)]), inc=sVE)
                    P.wait("tensor", sVE)
                if dc == 0:
                    tr_evac_mark = {}
                tr_evac_mark[dc] = sVE.n

        with nc.sbuf_tensor("wgsl", [128, 2 * 512], BF16) as wgsl, \
             nc.psum_tensor("ppc0", [128, 512], F32) as ppc0, \
             nc.psum_tensor("ppc1", [128, 512], F32) as ppc1, \
             nc.psum_tensor("ppc2", [128, 512], F32) as ppc2, \
             nc.psum_tensor("ppc3", [128, 512], F32) as ppc3, \
             nc.psum_tensor("ppb0", [128, 4], F32) as ppb0, \
             nc.psum_tensor("ppb1", [128, 4], F32) as ppb1, \
             nc.psum_tensor("ppb2", [128, 4], F32) as ppb2, \
             nc.psum_tensor("ppb3", [128, 4], F32) as ppb3:
            ppcs = [ppc0, ppc1, ppc2, ppc3]
            ppbs = [ppb0, ppb1, ppb2, ppb3]
            wc_mark = {}
            for half in range(2 if not SKIP_WC else 0):
                for fgrp in range(4):
                    for fp in range(16):
                        gi = (half * 4 + fgrp) * 16 + fp
                        if gi >= 2:
                            P.wait("gpsimd", sPE, wc_mark[gi - 2])
                        wsem = sWGa if fp % 2 == 0 else sWGb
                        P.dma("gpsimd", wgsl[:, 512 * (fp % 2):512 * (fp % 2 + 1)],
                              W_g[128 * fp:128 * (fp + 1),
                                  512 * fgrp:512 * (fgrp + 1)], wsem)
                        P.wait("tensor", wsem)
                        for dtl in range(4):
                            dt = 4 * half + dtl
                            last = (dtl == 3) and (half == 1)
                            P.op("tensor", lambda e, fp=fp, dtl=dtl, dt=dt: e.matmul(
                                ppcs[dtl][:],
                                winT[:, D * fp + 128 * dt:D * fp + 128 * (dt + 1)],
                                wgsl[:, 512 * (fp % 2):512 * (fp % 2 + 1)],
                                start=(fp == 0), stop=(fp == 15)),
                                inc=(sPE if last else None))
                        if half == 0:
                            for sub in range(4):
                                P.op("tensor", lambda e, fp=fp, sub=sub, fgrp=fgrp:
                                     e.matmul(
                                         ppbs[sub][:, fgrp:fgrp + 1],
                                         wgsl[:, 512 * (fp % 2) + 128 * sub:
                                              512 * (fp % 2) + 128 * (sub + 1)],
                                         bsml[:, fp:fp + 1],
                                         start=(fp == 0), stop=(fp == 15)),
                                     inc=(sPE if sub == 3 else None))
                        wc_mark[gi] = sPE.n
                    P.wait("vector", sPE)
                    for dtl in range(4):
                        dt = 4 * half + dtl
                        P.op("vector", lambda e, dt=dt, dtl=dtl, fgrp=fgrp: e.tensor_copy(
                            cw_b[:, FF * dt + 512 * fgrp:FF * dt + 512 * (fgrp + 1)],
                            ppcs[dtl][:]), inc=sVE)
                    if half == 0:
                        for sub in range(4):
                            ftcol = 4 * fgrp + sub
                            P.op("vector", lambda e, ftcol=ftcol, sub=sub, fgrp=fgrp:
                                 e.tensor_add(
                                     bcT_b[:, ftcol:ftcol + 1],
                                     ppbs[sub][:, fgrp:fgrp + 1],
                                     bgf[:, ftcol:ftcol + 1]), inc=sVE)
                    P.wait("tensor", sVE)

        P.quiesce(sBar, [sLD, sCP, sST, sWO, sWOb, sX, sXa, sXb, sWGa, sWGb, sTQ, sW13, sW2, sRS, sTS])
        with nc.sbuf_tensor("wof", [128, 2 * D], F32) as wof:
            for fc in range(16):
                if fc >= 2:
                    P.wait("sync", sVE, wo_mark[fc - 2])
                wsem = sWO if fc % 2 == 0 else sWOb
                P.dma("sync", wof[:, D * (fc % 2):D * (fc % 2 + 1)],
                      W_out[128 * fc:128 * (fc + 1), :], wsem)
                P.wait("vector", wsem)
                P.wait("vector", sLD)
                P.op("vector", lambda e, fc=fc: e.tensor_scalar_mul(
                    wo_b[:, D * fc:D * (fc + 1)], wof[:, D * (fc % 2):D * (fc % 2 + 1)],
                    bspf[:, fc:fc + 1]), inc=sVE)
                if fc == 0:
                    wo_mark = {}
                wo_mark[fc] = sVE.n
    P.quiesce(sBar, [sLD, sCP, sST, sWO, sWOb, sX, sXa, sXb, sWGa, sWGb, sTQ, sW13, sW2, sRS, sTS])

    x0c = [mk(A, f"x0c{i}", [128, 4096], F32) for i in range(2)]
    xnT = [mk(A, f"xnT{i}", [128, 4096], BF16) for i in range(2)]
    gTt = mk(A, "gTt", [128, 16 * 512], BF16)

    aob = mk(A, "aob", [128, 1024], F32)

    xnb_s = mk(A, "xnb_s", [128, 4096], BF16)
    x1c = mk(A, "x1c", [128, 1024], F32)
    pG = [mk(A, f"pG{i}", [128, 512], psum=True) for i in range(2)]
    pA = [mk(A, f"pA{i}", [128, 512], psum=True) for i in range(4)]
    pT = mk(A, "pT", [128, 512], BF16, psum=True)

    def ln_stats(tile_ap, scol, scratch, pre_sum=None):
        if pre_sum is None:
            P.op("vector", lambda e: e.reduce_sum(st1[:, scol:scol + 1], tile_ap, AX.X),
                 inc=sVE)
        else:
            P.op("vector", lambda e: e.tensor_copy(st1[:, scol:scol + 1], pre_sum),
                 inc=sVE)
        P.op("vector", lambda e: e.scalar_tensor_tensor(
            scratch, tile_ap, 1.0, tile_ap, ALU.mult, ALU.mult,
            accum_out=st1[:, scol + 1:scol + 2]), inc=sVE)
        P.wait("vector", sVE)
        P.op("vector", lambda e: e.tensor_scalar_mul(
            st1[:, scol:scol + 1], st1[:, scol:scol + 1], 1.0 / D), inc=sVE)
        P.op("vector", lambda e: e.tensor_scalar_mul(
            st1[:, scol + 1:scol + 2], st1[:, scol + 1:scol + 2], 1.0 / D), inc=sVE)
        P.wait("vector", sVE)
        P.op("vector", lambda e: e.scalar_tensor_tensor(
            st1[:, scol + 2:scol + 3], st1[:, scol:scol + 1], 1.0,
            st1[:, scol:scol + 1], ALU.mult, ALU.mult), inc=sVE)
        P.wait("vector", sVE)
        P.op("vector", lambda e: e.tensor_sub(
            st1[:, scol + 1:scol + 2], st1[:, scol + 1:scol + 2],
            st1[:, scol + 2:scol + 3]), inc=sVE)
        P.wait("scalar", sVE)
        P.op("scalar", lambda e: e.activation(
            st1[:, scol + 2:scol + 3], st1[:, scol + 1:scol + 2], AF.Sqrt,
            bias=epsb[:]), inc=sSE)
        P.wait("vector", sSE)
        P.op("vector", lambda e: e.reciprocal(
            st1[:, scol + 1:scol + 2], st1[:, scol + 2:scol + 3]), inc=sVE)
        P.wait("vector", sVE)
        P.op("vector", lambda e: e.scalar_tensor_tensor(
            st1[:, scol:scol + 1], st1[:, scol:scol + 1], -1.0,
            st1[:, scol + 1:scol + 2], ALU.mult, ALU.mult), inc=sVE)
        P.wait("vector", sVE)

    st_mark = {}
    tr_mark = {}
    silu_mark = {}
    for c in range(NCH if not globals().get('SKIP_P1') else 0):
        xb = x0c[c % 2]
        xT = xnT[c % 2]
        if c >= 2:
            P.wait("sync", sST, st_mark[c - 2])
        xsem = sXa if c % 2 == 0 else sXb
        for tt in range(4):
            P.dma("sync", xb[:, 1024 * tt:1024 * (tt + 1)],
                  x_in[512 * c + 128 * tt:512 * c + 128 * (tt + 1), :], xsem)
        P.wait("vector", xsem)
        P.wait("scalar", xsem)
        if c >= 1:
            P.wait("scalar", sPE, tr_mark[c - 1])
        for tt in range(4):
            xt = xb[:, 1024 * tt:1024 * (tt + 1)]
            sc = 16 * tt
            ln_stats(xt, sc, sc1[:])
            P.wait("scalar", sVE)
            P.op("scalar", lambda e, xt=xt, tt=tt, sc=sc: e.activation(
                xnb_s[:, 1024 * tt:1024 * (tt + 1)], xt, AF.Identity,
                bias=st1[:, sc:sc + 1], scale=st1[:, sc + 1:sc + 2]), inc=sSE)
        P.wait("tensor", sSE)
        for tt in range(4):
            for j in range(8):
                P.op("tensor", lambda e, tt=tt, j=j: e.transpose(
                    pT[:, 128 * (j % 4):128 * (j % 4 + 1)],
                    xnb_s[:, 1024 * tt + 128 * j:1024 * tt + 128 * (j + 1)],
                    eye_b[:]), inc=sPE)
                P.wait("vector", sPE)
                P.op("vector", lambda e, tt=tt, j=j, xT=xT: e.tensor_copy(
                    xT[:, 512 * j + 128 * tt:512 * j + 128 * (tt + 1)],
                    pT[:, 128 * (j % 4):128 * (j % 4 + 1)]), inc=sVE)
                P.wait("tensor", sVE)
        tr_mark[c] = sPE.n
        for ft in range(16):
            bk = pG[ft % 2]
            if ft >= 2:
                P.wait("tensor", sSE, silu_mark[ft - 2])
            elif c >= 1:
                P.wait("tensor", sSE, silu_mark[14 + ft])
            for dc in range(8):
                P.op("tensor", lambda e, ft=ft, dc=dc, bk=bk, xT=xT: e.matmul(
                    bk[:], cw_b[:, FF * dc + 128 * ft:FF * dc + 128 * (ft + 1)],
                    xT[:, 512 * dc:512 * (dc + 1)],
                    start=(dc == 0), stop=(dc == 7)),
                    inc=(sPE if dc == 7 else None))
            P.wait("scalar", sPE)
            P.op("scalar", lambda e, ft=ft, bk=bk: e.activation(
                gTt[:, 512 * ft:512 * (ft + 1)], bk[:], AF.Silu,
                bias=bcT_b[:, ft:ft + 1]), inc=sSE)
            silu_mark[ft] = sSE.n
        P.wait("tensor", sSE)
        for tt in range(4):
            for half in range(2):
                bk = pA[(2 * tt + half) % 4]
                for ft in range(16):
                    P.op("tensor", lambda e, tt=tt, half=half, ft=ft, bk=bk: e.matmul(
                        bk[:], gTt[:, 512 * ft + 128 * tt:512 * ft + 128 * (tt + 1)],
                        wo_b[:, D * ft + 512 * half:D * ft + 512 * (half + 1)],
                        start=(ft == 0), stop=(ft == 15)),
                        inc=(sPE if ft == 15 else None))
            P.wait("vector", sPE)
            xt = xb[:, 1024 * tt:1024 * (tt + 1)]
            sc = 16 * tt
            tmp, t1 = sc1[:], sc2[:]
            for half in range(2):
                bk = pA[(2 * tt + half) % 4]
                P.op("vector", lambda e, half=half, bk=bk: e.tensor_add(
                    aob[:, 512 * half:512 * (half + 1)], bk[:],
                    bout_b[:, 512 * half:512 * (half + 1)]), inc=sVE)
            P.wait("tensor", sVE)
            P.wait("scalar", sVE)
            P.op("scalar", lambda e, xt=xt, tmp=tmp, sc=sc: e.activation(
                tmp, xt, AF.Identity, bias=st1[:, sc:sc + 1], scale=st1[:, sc + 1:sc + 2]),
                inc=sSE)
            P.wait("vector", sSE)
            P.op("vector", lambda e, t1=t1, tmp=tmp, sc=sc: e.scalar_tensor_tensor(
                t1, tmp, 1.0, aob[:], ALU.mult, ALU.add,
                accum_out=st1[:, sc + 3:sc + 4]), inc=sVE)
            P.wait("vector", sVE)
            ln_stats(t1, sc + 4, tmp, pre_sum=st1[:, sc + 3:sc + 4])
            P.wait("scalar", sVE)
            P.op("scalar", lambda e, t1=t1, tmp=tmp, sc=sc: e.activation(
                tmp, t1, AF.Identity, bias=st1[:, sc + 4:sc + 5],
                scale=st1[:, sc + 5:sc + 6]), inc=sSE)
            P.wait("vector", sSE)
            P.op("vector", lambda e, tmp=tmp, t1=t1: e.scalar_tensor_tensor(
                t1, tmp, 1.0, nw_b[:], ALU.mult, ALU.mult), inc=sVE)
            P.wait("vector", sVE)
            P.op("vector", lambda e, t1=t1, sc=sc: e.scalar_tensor_tensor(
                t1, t1, 1.0, nb_b[:], ALU.mult, ALU.add,
                accum_out=st1[:, sc + 3:sc + 4]), inc=sVE)
            P.wait("vector", sVE)
            ln_stats(t1, sc + 8, tmp, pre_sum=st1[:, sc + 3:sc + 4])
            P.wait("scalar", sVE)
            P.op("scalar", lambda e, t1=t1, tmp=tmp, sc=sc: e.activation(
                tmp, t1, AF.Identity, bias=st1[:, sc + 8:sc + 9],
                scale=st1[:, sc + 9:sc + 10]), inc=sSE)
            P.wait("vector", sSE)
            P.op("vector", lambda e, tmp=tmp, xt=xt: e.tensor_add(x1c[:], tmp, xt),
                 inc=sVE)
            P.wait("sync", sVE)
            P.dma("sync", x1_d[512 * c + 128 * tt:512 * c + 128 * (tt + 1), :],
                  x1c[:], sST)
            if STAGE == 1:
                P.dma("sync", out_t[512 * c + 128 * tt:512 * c + 128 * (tt + 1), :],
                      x1c[:], sST)
            P.wait("vector", sST)
        st_mark[c] = sST.n

    if DBG and STAGE == 1:
        P.wait("gpsimd", sSE)
        P.wait("gpsimd", sVE)
        P.wait("gpsimd", sPE)
        P.dma("gpsimd", dbg["d_cw"], cw_b[:, 0:FF], sST)
        P.dma("gpsimd", dbg["d_wo"], wo_b[:, 0:D], sST)
        P.dma("gpsimd", dbg["d_bcT"], bcT_b[:], sST)
        P.wait("gpsimd", sST)
    if globals().get('DEVDUMP'):
        P.wait("gpsimd", sVE)
        P.wait("gpsimd", sPE)
        P.wait("gpsimd", sSE)
        P.wait("gpsimd", sST)
        P.dma("gpsimd", dbg["cwo"], cw_b[:], sST)
        P.dma("gpsimd", dbg["woo"], wo_b[:], sST)
        for rb in range(8):
            P.dma("gpsimd", dbg["x1o"][512 * rb:512 * (rb + 1), :],
                  x1_d[512 * rb:512 * (rb + 1), :], sST)
        P.wait("gpsimd", sST)
    P.quiesce(sBar, [sLD, sCP, sST, sWO, sWOb, sX, sXa, sXb, sWGa, sWGb, sTQ, sW13, sW2, sRS, sTS])
    A.close()

    if STAGE >= 2:
        B = ExitStack()
        QT_s = [mk(B, f"QT{g}", [128, S], BF16) for g in range(4)]
        KT_s = [mk(B, f"KT{g}", [128, S], BF16) for g in range(4)]
        Vp_s = mk(B, "Vp_s", [128, 33 * 16 * NT], BF16)
        ones_b = mk(B, "ones_b", [128, 128], BF16)
        st2 = mk(B, "st2", [128, 16], F32)
        B2 = ExitStack()
        x1b = [mk(B2, f"x1b{i}", [128, 4096], F32) for i in range(2)]
        xf_t = mk(B2, "xf_t", [128, 1024], F32)
        h_s = mk(B2, "h_s", [128, 4096], BF16)
        hT_s = mk(B2, "hT_s", [128, 4096], BF16)
        cT_s = mk(B2, "cT_s", [128, 4 * 512], BF16)
        qr_s = mk(B2, "qr_s", [128, 512], BF16)
        rsc1 = mk(B2, "rsc1", [128, 256], F32)
        rsc2 = mk(B2, "rsc2", [128, 256], F32)
        cs_s = mk(B2, "cs_s", [128, 64], F32)
        pQK = mk(B2, "pQK", [128, 512], psum=True)
        pT2 = mk(B2, "pT2", [128, 512], BF16, psum=True)
        pV = mk(B2, "pV", [128, 512], psum=True)
        P.op("gpsimd", lambda e: e.memset(ones_b[:], 1.0), inc=sCP)
        RC = 1.0 / (2.0 * np.pi)
        MAGIC = 12582912.0
        c1 = np.float32(6.28125)
        rem = 2.0 * np.pi - float(c1)
        c2 = np.float32(rem)
        c3 = np.float32(rem - float(c2))
        tr2_mark = {}
        ld2_mark = {}
        for c in range(NCH):
            xb = x1b[c % 2]
            if c >= 2:
                P.wait("sync", sVE, ld2_mark[c - 2])
            xsem = sXa if c % 2 == 0 else sXb
            for tt in range(4):
                P.dma("sync", xb[:, 1024 * tt:1024 * (tt + 1)],
                      x1_d[512 * c + 128 * tt:512 * c + 128 * (tt + 1), :], xsem)
            P.wait("vector", xsem)
            P.wait("scalar", xsem)
            if c >= 1:
                P.wait("scalar", sPE, tr2_mark[c - 1])
                P.wait("vector", sPE, tr2_mark[c - 1])
            for tt in range(4):
                xt = xb[:, 1024 * tt:1024 * (tt + 1)]
                ln_stats(xt, 0, xf_t[:])
                P.wait("scalar", sVE)
                P.op("scalar", lambda e, xt=xt: e.activation(
                    xf_t[:], xt, AF.Identity, bias=st1[:, 0:1], scale=st1[:, 1:2]),
                    inc=sSE)
                P.wait("vector", sSE)
                P.op("vector", lambda e: e.scalar_tensor_tensor(
                    sc1[:], xf_t[:], 1.0, xf_t[:], ALU.mult, ALU.mult,
                    accum_out=st2[:, 0:1]), inc=sVE)
                P.wait("vector", sVE)
                P.op("vector", lambda e: e.tensor_scalar_mul(
                    st2[:, 0:1], st2[:, 0:1], 1.0 / D), inc=sVE)
                P.wait("scalar", sVE)
                P.op("scalar", lambda e: e.activation(
                    st2[:, 1:2], st2[:, 0:1], AF.Sqrt, bias=epsb[:]), inc=sSE)
                P.wait("vector", sSE)
                P.op("vector", lambda e: e.reciprocal(st2[:, 0:1], st2[:, 1:2]),
                     inc=sVE)
                P.wait("vector", sVE)
                P.op("vector", lambda e, tt=tt: e.scalar_tensor_tensor(
                    h_s[:, 1024 * tt:1024 * (tt + 1)], xf_t[:], st2[:, 0:1],
                    g1_b[:], ALU.mult, ALU.mult), inc=sVE)
            P.wait("tensor", sVE)
            for tt in range(4):
                for j in range(8):
                    P.op("tensor", lambda e, tt=tt, j=j: e.transpose(
                        pT2[:, 128 * (j % 4):128 * (j % 4 + 1)],
                        h_s[:, 1024 * tt + 128 * j:1024 * tt + 128 * (j + 1)],
                        eye_b[:]), inc=sPE)
                    P.wait("vector", sPE)
                    P.op("vector", lambda e, tt=tt, j=j: e.tensor_copy(
                        hT_s[:, 512 * j + 128 * tt:512 * j + 128 * (tt + 1)],
                        pT2[:, 128 * (j % 4):128 * (j % 4 + 1)]), inc=sVE)
                    P.wait("tensor", sVE)
            for ct in range(4):
                for dc in range(8):
                    P.op("tensor", lambda e, ct=ct, dc=dc: e.matmul(
                        pQK[:], wdn_b[:, CC * dc + 128 * ct:CC * dc + 128 * (ct + 1)],
                        hT_s[:, 512 * dc:512 * (dc + 1)],
                        start=(dc == 0), stop=(dc == 7)),
                        inc=(sPE if dc == 7 else None))
                P.wait("vector", sPE)
                P.op("vector", lambda e, ct=ct: e.tensor_copy(
                    cT_s[:, 512 * ct:512 * (ct + 1)], pQK[:]), inc=sVE)
                P.wait("tensor", sVE)
            for tt in range(4):
                gt = 4 * c + tt
                P.wait("vector", sSE)
                P.op("vector", lambda e, gt=gt: e.tensor_scalar(
                    st2[:, 2:3], iotaP[:], float(128 * gt), None, ALU.add), inc=sVE)
                P.wait("vector", sVE)
                P.op("vector", lambda e: e.tensor_scalar(
                    cs_s[:, 0:16], invf_b[:], st2[:, 2:3], None, ALU.mult), inc=sVE)
                P.wait("vector", sVE)
                P.op("vector", lambda e: e.tensor_scalar(
                    cs_s[:, 16:32], cs_s[:, 0:16], RC, MAGIC, ALU.mult, ALU.add),
                    inc=sVE)
                P.wait("vector", sVE)
                P.op("vector", lambda e: e.tensor_scalar(
                    cs_s[:, 16:32], cs_s[:, 16:32], -MAGIC, None, ALU.add), inc=sVE)
                P.wait("vector", sVE)
                for cc_ in (float(c1), float(c2), float(c3)):
                    P.op("vector", lambda e, cc_=cc_: e.scalar_tensor_tensor(
                        cs_s[:, 0:16], cs_s[:, 16:32], -cc_, cs_s[:, 0:16],
                        ALU.mult, ALU.add), inc=sVE)
                    P.wait("vector", sVE)
                P.op("vector", lambda e: e.tensor_scalar(
                    cs_s[:, 16:32], cs_s[:, 0:16], float(np.pi / 2), None, ALU.add),
                    inc=sVE)
                P.wait("vector", sVE)
                P.op("vector", lambda e: e.tensor_scalar(
                    cs_s[:, 32:48], cs_s[:, 16:32], float(np.pi), None, ALU.is_gt),
                    inc=sVE)
                P.wait("vector", sVE)
                P.op("vector", lambda e: e.scalar_tensor_tensor(
                    cs_s[:, 16:32], cs_s[:, 32:48], float(-2.0 * np.pi),
                    cs_s[:, 16:32], ALU.mult, ALU.add), inc=sVE)
                P.wait("scalar", sVE)
                P.op("scalar", lambda e: e.activation(
                    cs_s[:, 32:48], cs_s[:, 0:16], AF.Sin), inc=sSE)
                P.op("scalar", lambda e: e.activation(
                    cs_s[:, 48:64], cs_s[:, 16:32], AF.Sin), inc=sSE)
                P.wait("vector", sSE)
                sinB = cs_s[:, 32:48].rearrange("p (o i) -> p o i", o=1
                                                ).broadcast_to([128, 16, 16])
                cosB = cs_s[:, 48:64].rearrange("p (o i) -> p o i", o=1
                                                ).broadcast_to([128, 16, 16])
                for side, wb in (("q", wq_b), ("k", wk_b)):
                    for cc in range(4):
                        P.op("tensor", lambda e, tt=tt, cc=cc, wb=wb: e.matmul(
                            pQK[:], cT_s[:, 512 * cc + 128 * tt:512 * cc + 128 * (tt + 1)],
                            wb[:, CC * cc:CC * (cc + 1)],
                            start=(cc == 0), stop=(cc == 3)),
                            inc=(sPE if cc == 3 else None))
                    P.wait("vector", sPE)
                    pv = pQK[:].rearrange("p (h i two) -> p h i two", h=16, two=2)
                    t1 = pv[:, :, :, 0:1].squeeze(3)
                    t2 = pv[:, :, :, 1:2].squeeze(3)
                    r1v = rsc1[:].rearrange("p (h i) -> p h i", h=16)
                    r2v = rsc2[:].rearrange("p (h i) -> p h i", h=16)
                    qv = qr_s[:].rearrange("p (h i two) -> p h i two", h=16, two=2)
                    q1 = qv[:, :, :, 0:1].squeeze(3)
                    q2 = qv[:, :, :, 1:2].squeeze(3)
                    P.op("vector", lambda e, t1=t1, cosB=cosB, r1v=r1v: e.tensor_tensor(
                        r1v, t1, cosB, ALU.mult), inc=sVE)
                    P.op("vector", lambda e, t2=t2, sinB=sinB, r2v=r2v: e.tensor_tensor(
                        r2v, t2, sinB, ALU.mult), inc=sVE)
                    P.wait("vector", sVE)
                    P.op("vector", lambda e, q1=q1, r1v=r1v, r2v=r2v: e.tensor_sub(
                        q1, r1v, r2v), inc=sVE)
                    P.op("vector", lambda e, t1=t1, sinB=sinB, r1v=r1v: e.tensor_tensor(
                        r1v, t1, sinB, ALU.mult), inc=sVE)
                    P.op("vector", lambda e, t2=t2, cosB=cosB, r2v=r2v: e.tensor_tensor(
                        r2v, t2, cosB, ALU.mult), inc=sVE)
                    P.wait("vector", sVE)
                    P.op("vector", lambda e, q2=q2, r1v=r1v, r2v=r2v: e.tensor_add(
                        q2, r1v, r2v), inc=sVE)
                    P.wait("tensor", sVE)
                    dstT = QT_s if side == "q" else KT_s
                    for g in range(4):
                        P.op("tensor", lambda e, g=g: e.transpose(
                            pT2[:, 128 * (g % 4):128 * (g % 4 + 1)],
                            qr_s[:, 128 * g:128 * (g + 1)], eye_b[:]), inc=sPE)
                        P.wait("vector", sPE)
                        P.op("vector", lambda e, g=g, tt=tt, c=c, dstT=dstT: e.tensor_copy(
                            dstT[g][:, 512 * c + 128 * tt:512 * c + 128 * (tt + 1)],
                            pT2[:, 128 * (g % 4):128 * (g % 4 + 1)]), inc=sVE)
                        P.wait("tensor", sVE)
                for cc in range(4):
                    P.op("tensor", lambda e, tt=tt, cc=cc: e.matmul(
                        pV[:], cT_s[:, 512 * cc + 128 * tt:512 * cc + 128 * (tt + 1)],
                        wv_b[:, CC * cc:CC * (cc + 1)],
                        start=(cc == 0), stop=(cc == 3)),
                        inc=(sPE if cc == 3 else None))
                P.wait("vector", sPE)
                vsrc = pV[:].rearrange("p (h i) -> p h i", h=16)
                vdst = Vp_s[:, 528 * gt:528 * (gt + 1)].rearrange(
                    "p (h i) -> p h i", i=33)[:, :, 0:32]
                P.op("vector", lambda e, vsrc=vsrc, vdst=vdst: e.tensor_copy(
                    vdst, vsrc), inc=sVE)
                od = Vp_s[:, 528 * gt:528 * (gt + 1)].rearrange(
                    "p (h i) -> p h i", i=33)[:, :, 32:33]
                P.op("vector", lambda e, od=od: e.memset(od, 1.0), inc=sVE)
                P.wait("tensor", sVE)
            tr2_mark[c] = sPE.n
            ld2_mark[c] = sVE.n
        P.wait("sync", sX)
        P.wait("sync", sST)
        P.quiesce(sBar, [sLD, sCP, sST, sWO, sWOb, sX, sXa, sXb, sWGa, sWGb, sTQ, sW13, sW2, sRS, sTS])
        B2.close()

        if DBG and STAGE == 2:
            P.wait("gpsimd", sVE)
            P.wait("gpsimd", sSE)
            P.wait("gpsimd", sPE)
            P.dma("gpsimd", dbg["d_qt"], QT_s[0][:, 0:1024], sST)
            P.dma("gpsimd", dbg["d_kt"], KT_s[3][:, 0:1024], sST)
            P.dma("gpsimd", dbg["d_vp"], Vp_s[:, 0:1056], sST)
            P.dma("gpsimd", dbg["d_ct"], cT_s[:, 0:2048], sST)
            P.wait("gpsimd", sST)

    if STAGE >= 3:
        o_nb = mk(B, "o_nb", [128, 512], BF16)
        eyef_b = mk(B, "eyef_b", [33, 33], F32)
        P.dma("sync", eyef_b[:], eyef_c, sLD)
        P.wait("tensor", sLD)
        OT_sb = mk(B, "OT_sb", [33, 4 * 512], F32)
        nmb = mk(B, "nmb", [128, 32], BF16)
        negm_s = mk(B, "negm_s", [128, 512], BF16)
        PT_b = mk(B, "PT_b", [128, 4 * 1024], BF16)
        rq_s4 = mk(B, "rq_s4", [128, 4], F32)
        mx2 = mk(B, "mx2", [128, 32], F32)
        pMA = [mk(B, f"pMA{i}", [128, 1024], psum=True) for i in range(2)]
        pSp = mk(B, "pSp", [128, 1024], psum=True)
        pAV = [mk(B, f"pAV{i}", [128, 512], psum=True) for i in range(2)]

        units = [(g, qc) for g in range(4) for qc in range(NCH)]
        mx_rd_mark = {}
        mx_done = {}
        kt_se_done = {}
        unit_ve_done = {}

        def max_steps(u):
            g, qc = units[u]
            base = 16 * (u % 2)
            sidx = 0
            smark = {}
            for qt in range(4):
                for kc in range(qc + 1):
                    for pr in range(2):
                        bank = pMA[sidx % 2]
                        if sidx >= 2:
                            P.wait("tensor", sVE, smark[sidx - 2])
                        elif u >= 1:
                            P.wait("tensor", sVE, mx_done[u - 1])
                        for hl in range(2):
                            hj = 2 * pr + hl
                            P.op("tensor", lambda e, g=g, qc=qc, qt=qt, kc=kc, hj=hj,
                                 bank=bank, hl=hl: e.matmul(
                                     bank[:, 512 * hl:512 * (hl + 1)],
                                     QT_s[g][32 * hj:32 * (hj + 1),
                                             512 * qc + 128 * qt:512 * qc + 128 * (qt + 1)],
                                     KT_s[g][32 * hj:32 * (hj + 1),
                                             512 * kc:512 * (kc + 1)],
                                     start=True, stop=True,
                                     tile_position=(32 * hj, 0)),
                                 inc=(sPE if hl == 1 else None))
                        P.wait("vector", sPE)
                        for hl in range(2):
                            hj = 2 * pr + hl
                            P.op("vector", lambda e, hj=hj, kc=kc, bank=bank, hl=hl:
                                 e.reduce_max(mx2[:, 8 * hj + kc:8 * hj + kc + 1],
                                              bank[:, 512 * hl:512 * (hl + 1)],
                                              AX.X, negate=True), inc=sVE)
                        smark[sidx] = sVE.n
                        sidx += 1
                        yield
                P.wait("vector", sVE)
                for hj in range(4):
                    col = base + 4 * hj + qt
                    P.op("vector", lambda e, hj=hj, col=col, qc=qc: e.tensor_reduce(
                        nmb[:, col:col + 1], mx2[:, 8 * hj:8 * hj + qc + 1], AX.X,
                        ALU.min), inc=sVE)
                yield
            mx_done[u] = sVE.n
            while True:
                yield

        def pump(gen, n=1):
            if gen is None:
                return
            for _ in range(n):
                try:
                    next(gen)
                except StopIteration:
                    pass

        def emit_unit(u, nxt_gen):
            g, qc = units[u]
            base = 16 * (u % 2)
            P.wait("tensor", sVE, mx_done[u])
            if u >= 1:
                P.wait("tensor", sVE, unit_ve_done[u - 1])
                P.wait("tensor", sSE, kt_se_done[u - 1])
            for qt in range(4):
                for hj in range(4):
                    P.op("tensor", lambda e, qt=qt, hj=hj, base=base: e.matmul(
                        pSp[32 * hj:32 * hj + 1, 128 * qt:128 * (qt + 1)],
                        nmb[:, base + 4 * hj + qt:base + 4 * hj + qt + 1], eye_b[:],
                        start=True, stop=True,
                        tile_position=(0, 32 * hj)), inc=sPE)
            P.wait("vector", sPE)
            P.op("vector", lambda e: e.tensor_copy(negm_s[:], pSp[:, 0:512]), inc=sVE)
            P.wait("tensor", sVE)
            nkt = 4 * qc + 4
            for pr in range(2):
                exp_mark = {}
                msk_mark = {}
                av_mark = {}
                for kt in range(nkt):
                    if kt >= 1:
                        P.wait("tensor", sSE, exp_mark[kt - 1])
                    for hl in range(2):
                        hj = 2 * pr + hl
                        P.op("tensor", lambda e, g=g, qc=qc, kt=kt, hj=hj, hl=hl:
                             e.matmul(
                                 pSp[:, 512 * hl:512 * (hl + 1)],
                                 KT_s[g][32 * hj:32 * (hj + 1),
                                         128 * kt:128 * (kt + 1)],
                                 QT_s[g][32 * hj:32 * (hj + 1),
                                         512 * qc:512 * (qc + 1)],
                                 start=True, stop=False,
                                 tile_position=(32 * hj, 0)), inc=None)
                        P.op("tensor", lambda e, hj=hj, hl=hl: e.matmul(
                            pSp[:, 512 * hl:512 * (hl + 1)],
                            ones_b[32 * hj:32 * hj + 1, 0:128],
                            negm_s[32 * hj:32 * hj + 1, :],
                            start=False, stop=True,
                            tile_position=(32 * hj, 0)),
                            inc=(sPE if hl == 1 else None))
                    st_mark_kt = sPE.n
                    P.wait("scalar", sPE, st_mark_kt)
                    if kt >= 4:
                        P.wait("scalar", sPE, av_mark[kt - 4])
                    P.op("scalar", lambda e, kt=kt: e.activation(
                        PT_b[:, 1024 * (kt % 4):1024 * (kt % 4 + 1)], pSp[:], AF.Exp),
                        inc=sSE)
                    exp_mark[kt] = sSE.n
                    kt_rel = kt - 4 * qc
                    if kt_rel >= 0:
                        P.wait("vector", sSE, exp_mark[kt])
                        pt3 = PT_b[:, 1024 * (kt % 4):1024 * (kt % 4 + 1)].rearrange(
                            "p (two q) -> p two q", two=2)
                        bm3 = bm_b[:, 512 * kt_rel:512 * (kt_rel + 1)].rearrange(
                            "p (o q) -> p o q", o=1).broadcast_to([128, 2, 512])
                        P.op("vector", lambda e, pt3=pt3, bm3=bm3: e.tensor_tensor(
                            pt3, pt3, bm3, ALU.mult), inc=sVE)
                        msk_mark[kt] = sVE.n
                    for hl in range(2):
                        hj = 2 * pr + hl
                        if hl == 0:
                            if kt_rel >= 0:
                                P.wait("tensor", sVE, msk_mark[kt])
                            else:
                                P.wait("tensor", sSE, exp_mark[kt])
                        P.op("tensor", lambda e, g=g, kt=kt, hj=hj, hl=hl, nkt=nkt:
                             e.matmul(
                                 pAV[hl][0:33, :],
                                 Vp_s[:, 528 * kt + 33 * (4 * g + hj):
                                      528 * kt + 33 * (4 * g + hj) + 33],
                                 PT_b[:, 1024 * (kt % 4) + 512 * hl:
                                      1024 * (kt % 4) + 512 * (hl + 1)],
                                 start=(kt == 0), stop=(kt == nkt - 1)),
                             inc=(sPE if hl == 1 else None))
                    av_mark[kt] = sPE.n
                    pump(nxt_gen)
                P.wait("vector", sPE, av_mark[nkt - 1])
                for hl in range(2):
                    P.op("vector", lambda e, pr=pr, hl=hl: e.tensor_copy(
                        OT_sb[:, 1024 * pr + 512 * hl:1024 * pr + 512 * (hl + 1)],
                        pAV[hl][0:33, :]), inc=sVE)
                P.wait("tensor", sVE)
            kt_se_done[u] = sSE.n
            for qt in range(4):
                for hj in range(4):
                    P.op("tensor", lambda e, hj=hj, qt=qt: e.transpose(
                        pAV[0][:, 33 * hj:33 * (hj + 1)],
                        OT_sb[:, 512 * hj + 128 * qt:512 * hj + 128 * (qt + 1)],
                        eyef_b[:]), inc=(sPE if hj == 3 else None))
                P.wait("vector", sPE)
                P.op("vector", lambda e: e.reciprocal(
                    rq_s4[:], pAV[0][:, 0:132].rearrange(
                        "p (h o) -> p h o", o=33)[:, :, 32:33].squeeze(2)), inc=sVE)
                P.wait("vector", sVE)
                for hj in range(4):
                    P.op("vector", lambda e, g=g, hj=hj: e.tensor_scalar_mul(
                        o_nb[:, 128 * g + 32 * hj:128 * g + 32 * (hj + 1)],
                        pAV[0][:, 33 * hj:33 * hj + 32], rq_s4[:, hj:hj + 1]),
                        inc=sVE)
                P.wait("tensor", sVE)
                P.wait("sync", sVE)
                qi = 4 * qc + qt
                P.dma("sync", o_d[128 * qi:128 * (qi + 1),
                                  128 * g:128 * (g + 1)],
                      o_nb[:, 128 * g:128 * (g + 1)], sST)
                P.wait("vector", sST)
                pump(nxt_gen)
            unit_ve_done[u] = sVE.n

        g0 = max_steps(0)
        pump(g0, 200)
        for u in range(len(units)):
            gn = max_steps(u + 1) if u + 1 < len(units) else None
            emit_unit(u, gn)
            pump(gn, 200)
        P.wait("sync", sST)
        if DBG and STAGE == 3:
            P.wait("gpsimd", sVE)
            P.wait("gpsimd", sSE)
            P.wait("gpsimd", sPE)
            P.wait("gpsimd", sST)
            P.dma("gpsimd", dbg["d_o"], o_d[0:128, :], sST)
            P.dma("gpsimd", dbg["d_tt"][:, 0:512], o_d[S - 128:S, 0:512], sST)
            P.wait("gpsimd", sST)
    if STAGE >= 2:
        P.quiesce(sBar, [sLD, sCP, sST, sWO, sWOb, sX, sXa, sXb, sWGa, sWGb, sTQ, sW13, sW2, sRS, sTS])
        B.close()

    if STAGE >= 4:
        C = ExitStack()
        oc_s = mk(C, "oc_s", [128, 2048], BF16)
        oT_s = mk(C, "oT_s", [128, 2048], BF16)
        x1b4 = [mk(C, f"x1b4{i}", [128, 4096], F32) for i in range(2)]
        xf4 = mk(C, "xf4", [128, 1024], F32)
        x2_t = mk(C, "x2_t", [128, 1024], F32)
        t_s = mk(C, "t_s", [128, 4096], BF16)
        tT_sb = mk(C, "tT_sb", [128, 4096], BF16)
        ge_s = mk(C, "ge_s", [128, 8], F32)
        t8 = mk(C, "t8", [128, 40], F32)
        pUP = [mk(C, f"pUP{i}", [128, 512], psum=True) for i in range(2)]
        pT4 = mk(C, "pT4", [128, 512], BF16, psum=True)
        pRT = mk(C, "pRT", [128, 8], psum=True)
        ld4_mark = {}
        p4pe_mark = {}
        tts_mark = {}
        for c in range(NCH):
            xb = x1b4[c % 2]
            if c >= 2:
                P.wait("sync", sVE, ld4_mark[c - 2])
            if c >= 1:
                P.wait("sync", sPE, p4pe_mark[c - 1])
            xsem = sXa if c % 2 == 0 else sXb
            for tt in range(4):
                P.dma("sync", xb[:, 1024 * tt:1024 * (tt + 1)],
                      x1_d[512 * c + 128 * tt:512 * c + 128 * (tt + 1), :], xsem)
            for tt in range(4):
                P.dma("sync", oc_s[:, 512 * tt:512 * (tt + 1)],
                      o_d[512 * c + 128 * tt:512 * c + 128 * (tt + 1), :], xsem)
            P.wait("tensor", xsem)
            P.wait("vector", xsem)
            P.wait("scalar", xsem)
            if c >= 1:
                P.wait("vector", sPE, p4pe_mark[c - 1])
            for tt in range(4):
                for j in range(4):
                    P.op("tensor", lambda e, tt=tt, j=j: e.transpose(
                        pT4[:, 128 * (j % 4):128 * (j % 4 + 1)],
                        oc_s[:, 512 * tt + 128 * j:512 * tt + 128 * (j + 1)],
                        eye_b[:]), inc=sPE)
                    P.wait("vector", sPE)
                    P.op("vector", lambda e, tt=tt, j=j: e.tensor_copy(
                        oT_s[:, 512 * j + 128 * tt:512 * j + 128 * (tt + 1)],
                        pT4[:, 128 * (j % 4):128 * (j % 4 + 1)]), inc=sVE)
                    P.wait("tensor", sVE)
            for tt in range(4):
                gt = 4 * c + tt
                xt = xb[:, 1024 * tt:1024 * (tt + 1)]
                ln_stats(xt, 0, xf4[:])
                P.wait("scalar", sVE)
                P.op("scalar", lambda e, xt=xt: e.activation(
                    xf4[:], xt, AF.Identity, bias=st1[:, 0:1], scale=st1[:, 1:2]),
                    inc=sSE)
                P.wait("tensor", sSE)
                for half in range(2):
                    for cc in range(4):
                        P.op("tensor", lambda e, tt=tt, cc=cc, half=half: e.matmul(
                            pUP[half][:],
                            oT_s[:, 512 * cc + 128 * tt:512 * cc + 128 * (tt + 1)],
                            wup_b[:, D * cc + 512 * half:D * cc + 512 * (half + 1)],
                            start=(cc == 0), stop=(cc == 3)),
                            inc=(sPE if cc == 3 else None))
                P.wait("vector", sPE)
                P.wait("vector", sRS)
                for half in range(2):
                    P.op("vector", lambda e, half=half: e.tensor_add(
                        x2_t[:, 512 * half:512 * (half + 1)], pUP[half][:],
                        xf4[:, 512 * half:512 * (half + 1)]), inc=sVE)
                P.wait("tensor", sVE)
                P.wait("sync", sVE)
                P.dma("sync", res_d[512 * c + 128 * tt:512 * c + 128 * (tt + 1), :],
                      x2_t[:], sRS)
                P.op("vector", lambda e: e.scalar_tensor_tensor(
                    sc1[:], x2_t[:], 1.0, x2_t[:], ALU.mult, ALU.mult,
                    accum_out=st2[:, 0:1]), inc=sVE)
                P.wait("vector", sVE)
                P.op("vector", lambda e: e.tensor_scalar_mul(
                    st2[:, 0:1], st2[:, 0:1], 1.0 / D), inc=sVE)
                P.wait("scalar", sVE)
                P.op("scalar", lambda e: e.activation(
                    st2[:, 1:2], st2[:, 0:1], AF.Sqrt, bias=epsb[:]), inc=sSE)
                P.wait("vector", sSE)
                P.op("vector", lambda e: e.reciprocal(st2[:, 0:1], st2[:, 1:2]),
                     inc=sVE)
                P.wait("vector", sVE)
                P.op("vector", lambda e, tt=tt: e.scalar_tensor_tensor(
                    t_s[:, 1024 * tt:1024 * (tt + 1)], x2_t[:], st2[:, 0:1],
                    g2_b[:], ALU.mult, ALU.mult), inc=sVE)
            P.wait("tensor", sVE)
            P.wait("tensor", sTS, tts_mark.get(c - 1, 0))
            for tt in range(4):
                for j in range(8):
                    P.op("tensor", lambda e, tt=tt, j=j: e.transpose(
                        pT4[:, 128 * (j % 4):128 * (j % 4 + 1)],
                        t_s[:, 1024 * tt + 128 * j:1024 * tt + 128 * (j + 1)],
                        eye_b[:]), inc=sPE)
                    P.wait("vector", sPE)
                    P.op("vector", lambda e, tt=tt, j=j: e.tensor_copy(
                        tT_sb[:, 512 * j + 128 * tt:512 * j + 128 * (tt + 1)],
                        pT4[:, 128 * (j % 4):128 * (j % 4 + 1)]), inc=sVE)
                    P.wait("tensor", sVE)
            P.wait("sync", sVE)
            for dc in range(8):
                P.dma("sync", tT_d[128 * dc:128 * (dc + 1), 512 * c:512 * (c + 1)],
                      tT_sb[:, 512 * dc:512 * (dc + 1)], sTS)
            for tt in range(4):
                gt = 4 * c + tt
                for dc in range(8):
                    P.op("tensor", lambda e, tt=tt, dc=dc: e.matmul(
                        pRT[:], tT_sb[:, 512 * dc + 128 * tt:512 * dc + 128 * (tt + 1)],
                        rtr_b[:, E * dc:E * (dc + 1)],
                        start=(dc == 0), stop=(dc == 7)),
                        inc=(sPE if dc == 7 else None))
                P.wait("vector", sPE)
                P.op("vector", lambda e: e.reduce_max(
                    t8[:, 0:1], pRT[:], AX.X, negate=True), inc=sVE)
                P.wait("scalar", sVE)
                P.op("scalar", lambda e: e.activation(
                    ge_s[:], pRT[:], AF.Exp, bias=t8[:, 0:1]), inc=sSE)
                P.wait("tensor", sSE)
                P.wait("vector", sSE)
                P.op("vector", lambda e: e.tensor_scalar(
                    t8[:, 8:16], ge_s[:], 1.0, None, ALU.is_equal), inc=sVE)
                P.wait("vector", sVE)
                P.op("vector", lambda e: e.tensor_mul(
                    t8[:, 8:16], t8[:, 8:16], wcol_b[:]), inc=sVE)
                P.wait("vector", sVE)
                P.op("vector", lambda e: e.reduce_max(t8[:, 1:2], t8[:, 8:16], AX.X),
                     inc=sVE)
                P.wait("vector", sVE)
                P.op("vector", lambda e: e.tensor_scalar(
                    t8[:, 8:16], t8[:, 8:16], t8[:, 1:2], None, ALU.is_equal),
                    inc=sVE)
                P.wait("vector", sVE)
                P.op("vector", lambda e: e.scalar_tensor_tensor(
                    t8[:, 16:24], t8[:, 8:16], -1e30, ge_s[:], ALU.mult, ALU.add),
                    inc=sVE)
                P.wait("vector", sVE)
                P.op("vector", lambda e: e.reduce_max(t8[:, 2:3], t8[:, 16:24], AX.X),
                     inc=sVE)
                P.wait("vector", sVE)
                P.op("vector", lambda e: e.tensor_scalar(
                    t8[:, 24:32], t8[:, 16:24], t8[:, 2:3], None, ALU.is_equal),
                    inc=sVE)
                P.wait("vector", sVE)
                P.op("vector", lambda e: e.tensor_mul(
                    t8[:, 24:32], t8[:, 24:32], wcol_b[:]), inc=sVE)
                P.wait("vector", sVE)
                P.op("vector", lambda e: e.reduce_max(t8[:, 3:4], t8[:, 24:32], AX.X),
                     inc=sVE)
                P.wait("vector", sVE)
                P.op("vector", lambda e: e.tensor_scalar(
                    t8[:, 24:32], t8[:, 24:32], t8[:, 3:4], None, ALU.is_equal),
                    inc=sVE)
                P.wait("vector", sVE)
                P.op("vector", lambda e: e.tensor_add(
                    t8[:, 8:16], t8[:, 8:16], t8[:, 24:32]), inc=sVE)
                P.op("vector", lambda e: e.tensor_scalar(
                    t8[:, 4:5], t8[:, 2:3], 1.0, None, ALU.add), inc=sVE)
                P.wait("vector", sVE)
                P.op("vector", lambda e: e.reciprocal(t8[:, 5:6], t8[:, 4:5]), inc=sVE)
                P.wait("vector", sVE)
                P.op("vector", lambda e: e.tensor_mul(
                    t8[:, 8:16], t8[:, 8:16], ge_s[:]), inc=sVE)
                P.wait("vector", sVE)
                P.op("vector", lambda e, gt=gt: e.tensor_scalar_mul(
                    gate_b[:, E * gt:E * (gt + 1)], t8[:, 8:16], t8[:, 5:6]), inc=sVE)
            ld4_mark[c] = sVE.n
            p4pe_mark[c] = sPE.n
            tts_mark[c] = sTS.n
        P.wait("sync", sST)
        P.quiesce(sBar, [sLD, sCP, sST, sWO, sWOb, sX, sXa, sXb, sWGa, sWGb, sTQ, sW13, sW2, sRS, sTS])
        C.close()
        MID2.close()
        if DBG and STAGE == 4:
            P.dma("gpsimd", dbg["d_gate"], gate_b[:, 0:256], sST)
            P.dma("gpsimd", dbg["d_tt"], tT_d[0:128, 0:1024], sST)
            P.dma("gpsimd", dbg["d_res"], res_d[0:128, :], sST)
            P.wait("gpsimd", sST)

    if globals().get('DEVDUMP') and STAGE >= 4:
        P.wait("gpsimd", sVE)
        P.wait("gpsimd", sPE)
        P.wait("gpsimd", sSE)
        P.wait("gpsimd", sST)
        for rb in range(8):
            P.dma("gpsimd", dbg["oo"][512 * rb:512 * (rb + 1), :],
                  o_d[512 * rb:512 * (rb + 1), :], sST)
            P.dma("gpsimd", dbg["tto"][128 * rb:128 * (rb + 1), :],
                  tT_d[128 * rb:128 * (rb + 1), :], sST)
            P.dma("gpsimd", dbg["reso"][512 * rb:512 * (rb + 1), :],
                  res_d[512 * rb:512 * (rb + 1), :], sST)
        P.dma("gpsimd", dbg["gto"], gate_b[:], sST)
        P.wait("gpsimd", sST)

    if STAGE >= 5:
        Dx = ExitStack()
        w1e_s = mk(Dx, "w1e_s", [128, 8 * MF], BF16)
        w3e_s = mk(Dx, "w3e_s", [128, 8 * MF], BF16)
        w2e_s = mk(Dx, "w2e_s", [128, 16 * D], BF16)
        hidT_s = mk(Dx, "hidT_s", [128, 16 * 1024], BF16)
        acc_s = mk(Dx, "acc_s", [128, 8 * 1024], F32)
        tTq_s = mk(Dx, "tTq_s", [128, 8 * 1024], BF16)
        res_t = mk(Dx, "res_t", [128, 1024], F32)
        x1_t5 = mk(Dx, "x1_t5", [128, 1024], F32)
        f_t5 = mk(Dx, "f_t5", [128, 1024], F32)
        sil_s = mk(Dx, "sil_s", [128, 1024], BF16)
        pH = [mk(Dx, f"pH{i}", [128, 512], psum=True) for i in range(4)]
        pY = [mk(Dx, f"pY{i}", [128, 512], psum=True) for i in range(4)]
        wmk = {}
        w13_mark = {}
        w2_mark = {}

        def load_w13(e_, q4_):
            for dc in range(8):
                P.dma("gpsimd", w1e_s[:, MF * dc:MF * (dc + 1)],
                      w1[e_, 128 * dc:128 * (dc + 1), :], sW13)
                P.dma("gpsimd", w3e_s[:, MF * dc:MF * (dc + 1)],
                      w3[e_, 128 * dc:128 * (dc + 1), :], sW13)

        def load_w2(e_):
            for fc in range(16):
                P.dma("gpsimd", w2e_s[:, D * fc:D * (fc + 1)],
                      w2[e_, 128 * fc:128 * (fc + 1), :], sW2)

        load_w13(0, 0)
        w13_mark[(0, 0)] = sW13.n
        load_w2(0)
        w2_mark[(0, 0)] = sW2.n
        for q4 in range(4):
            if q4 >= 1:
                P.wait("sync", sPE, wmk["hid_end"])
            for dc in range(8):
                P.dma("sync", tTq_s[:, 1024 * dc:1024 * (dc + 1)],
                      tT_d[128 * dc:128 * (dc + 1), 1024 * q4:1024 * (q4 + 1)], sTQ)
            for e_ in range(E):
                P.wait("tensor", sW13, w13_mark[(q4, e_)])
                P.wait("tensor", sTQ)
                for ft in range(16):
                    for half in range(2):
                        if ft >= 1:
                            P.wait("tensor", sVE, wmk[("hid_ev", half)])
                        for dc in range(8):
                            P.op("tensor", lambda e, ft=ft, dc=dc, half=half: e.matmul(
                                pH[half][:],
                                w1e_s[:, MF * dc + 128 * ft:MF * dc + 128 * (ft + 1)],
                                tTq_s[:, 1024 * dc + 512 * half:1024 * dc + 512 * (half + 1)],
                                start=(dc == 0), stop=(dc == 7)), inc=None)
                        for dc in range(8):
                            P.op("tensor", lambda e, ft=ft, dc=dc, half=half: e.matmul(
                                pH[2 + half][:],
                                w3e_s[:, MF * dc + 128 * ft:MF * dc + 128 * (ft + 1)],
                                tTq_s[:, 1024 * dc + 512 * half:1024 * dc + 512 * (half + 1)],
                                start=(dc == 0), stop=(dc == 7)),
                                inc=(sPE if dc == 7 else None))
                        wmk[("hid_mm", half)] = sPE.n
                    for half in range(2):
                        P.wait("scalar", sPE, wmk[("hid_mm", half)])
                        if ft >= 1:
                            P.wait("scalar", sVE, wmk[("hid_ev", half)])
                        P.op("scalar", lambda e, half=half: e.activation(
                            sil_s[:, 512 * half:512 * (half + 1)], pH[half][:],
                            AF.Silu), inc=sSE)
                        wmk[("sil", half)] = sSE.n
                        P.wait("vector", sSE, wmk[("sil", half)])
                        P.op("vector", lambda e, ft=ft, half=half: e.tensor_tensor(
                            hidT_s[:, 1024 * ft + 512 * half:1024 * ft + 512 * (half + 1)],
                            sil_s[:, 512 * half:512 * (half + 1)],
                            pH[2 + half][:], ALU.mult), inc=sVE)
                        wmk[("hid_ev", half)] = sVE.n
                hid_mm_end = sPE.n
                wmk["hid_end"] = hid_mm_end
                P.wait("gpsimd", sPE, hid_mm_end)
                nxt = (q4, e_ + 1) if e_ + 1 < E else (q4 + 1, 0)
                if nxt[0] < 4:
                    load_w13(nxt[1], nxt[0])
                    w13_mark[nxt] = sW13.n
                P.wait("tensor", sVE)
                P.wait("tensor", sW2, w2_mark[(q4, e_)])
                for tt in range(8):
                    pyb = [pY[2 * (tt % 2)], pY[2 * (tt % 2) + 1]]
                    if tt >= 2:
                        P.wait("tensor", sVE, wmk[("y_ev", tt % 2)])
                    for ft in range(16):
                        for half in range(2):
                            P.op("tensor", lambda e, tt=tt, ft=ft, half=half, pyb=pyb:
                                 e.matmul(
                                     pyb[half][:],
                                     hidT_s[:, 1024 * ft + 128 * tt:1024 * ft + 128 * (tt + 1)],
                                     w2e_s[:, D * ft + 512 * half:D * ft + 512 * (half + 1)],
                                     start=(ft == 0), stop=(ft == 15)),
                                 inc=(sPE if (ft == 15 and half == 1) else None))
                    P.wait("vector", sPE)
                    gt = 8 * q4 + tt
                    for half in range(2):
                        dst = acc_s[:, 1024 * tt + 512 * half:1024 * tt + 512 * (half + 1)]
                        if e_ == 0:
                            P.op("vector", lambda e, dst=dst, half=half, gt=gt, pyb=pyb:
                                 e.tensor_scalar(
                                     dst, pyb[half][:],
                                     gate_b[:, E * gt + 0:E * gt + 1], None, ALU.mult),
                                 inc=sVE)
                        else:
                            P.op("vector", lambda e, dst=dst, half=half, gt=gt, e_=e_,
                                 pyb=pyb: e.scalar_tensor_tensor(
                                     dst, pyb[half][:],
                                     gate_b[:, E * gt + e_:E * gt + e_ + 1], dst,
                                     ALU.mult, ALU.add), inc=sVE)
                    wmk[("y_ev", tt % 2)] = sVE.n
                ymm_end = sPE.n
                P.wait("gpsimd", sPE, ymm_end)
                if nxt[0] < 4:
                    load_w2(nxt[1])
                    w2_mark[nxt] = sW2.n
            for tt in range(8):
                gq = 8 * q4 + tt
                P.wait("sync", sVE)
                P.dma("sync", res_t[:], res_d[128 * gq:128 * (gq + 1), :], sX)
                P.dma("sync", x1_t5[:], x1_d[128 * gq:128 * (gq + 1), :], sX)
                P.wait("vector", sX)
                P.op("vector", lambda e, tt=tt: e.tensor_add(
                    f_t5[:], acc_s[:, 1024 * tt:1024 * (tt + 1)], res_t[:]), inc=sVE)
                P.wait("vector", sVE)
                ln_stats(f_t5[:], 0, sc1[:])
                P.wait("scalar", sVE)
                P.op("scalar", lambda e: e.activation(
                    sc2[:], f_t5[:], AF.Identity, bias=st1[:, 0:1],
                    scale=st1[:, 1:2]), inc=sSE)
                P.wait("vector", sSE)
                P.op("vector", lambda e: e.tensor_add(f_t5[:], sc2[:], x1_t5[:]),
                     inc=sVE)
                P.wait("sync", sVE)
                P.dma("sync", out_t[128 * gq:128 * (gq + 1), :], f_t5[:], sST)
                P.wait("vector", sST)
                P.wait("sync", sST)
        P.wait("sync", sST)
        P.wait("sync", sX)
        P.wait("sync", sTQ)
        P.wait("sync", sW13)
        P.wait("sync", sW2)
        P.barrier(sBar)
        Dx.close()

    P.wait("sync", sST)
    P.wait("sync", sLD)
    P.wait("sync", sCP)
    P.barrier(sBar)
    if STAGE < 4:
        MID2.close()
    with nc.Block() as block:
        P.build(block)
    MID.close()
    G.close()
    return nc


def kernel(**inputs):
    xfull = np.ascontiguousarray(inputs["L0_attn_x"])
    nc = build_nc()
    common = {
        "W_in": inputs["L0_attn_W_in"], "b_in": inputs["L0_attn_b_in"],
        "b_sp": inputs["L0_attn_b_sp"], "W_g": inputs["L0_attn_W_g"],
        "b_g": inputs["L0_attn_b_g"], "W_out": inputs["L0_attn_W_out"],
        "b_out": inputs["L0_attn_b_out"], "norm_w": inputs["L0_attn_norm_w"],
        "norm_b": inputs["L0_attn_norm_b"], "g1": inputs["L0_ffn_g1"],
        "g2": inputs["L0_ffn_g2"], "w_down": inputs["L0_ffn_w_down"],
        "wq": inputs["L0_ffn_wq"], "wk": inputs["L0_ffn_wk"],
        "wv": inputs["L0_ffn_wv"], "w_up": inputs["L0_ffn_w_up"],
        "router_w": inputs["L0_ffn_router_w"], "w1": inputs["L0_ffn_w1"],
        "w2": inputs["L0_ffn_w2"], "w3": inputs["L0_ffn_w3"],
    }
    common = {k: np.ascontiguousarray(v) for k, v in common.items()}
    in_maps = [dict(common, x=np.ascontiguousarray(xfull[b])) for b in range(8)]
    res = run_bass_kernel_spmd(nc, in_maps, core_ids=list(range(8)))
    global LAST
    LAST = res
    return np.stack([res.results[b]["out"] for b in range(8)], axis=0)
